# version 25
# baseline (speedup 1.0000x reference)
import math
from contextlib import ExitStack

import numpy as np
import concourse.bass as bass
import concourse.mybir as mybir
from concourse.bass_utils import run_bass_kernel_spmd

F32 = mybir.dt.float32
BF16 = mybir.dt.bfloat16
AF = mybir.ActivationFunctionType
OP = mybir.AluOpType
AX = mybir.AxisListType

S = 2048
D = 4096
H = 32
KV = 8
HD = 128
ND = 32
NT = 16
HALF = 1024
E = 4
SYS = 35
IMG = 576
BOUND = SYS + IMG
C_INV = 1.0 / math.sqrt(HD)
EPS = 1e-5

CV_NEGHALF = 0
CV_NEG1 = 1
CV_FOUR = 2

PAIR_SCHED = []
for _pr in range(H // 2):
    for _qt in range(4):
        _bands = [(j, max(512 * _qt, 128 * j), 512 * (_qt + 1))
                  for j in range(4 * _qt + 4)]
        PAIR_SCHED.append((_pr, _qt, _bands))
BANDS_PER_PAIR = 4 + 8 + 12 + 16


class Ctr:

    def __init__(self, h):
        self.h = h
        self.v = 0

    def inc(self, instr, k=1):
        instr.then_inc(self.h, k)
        self.v += k
        return self.v


class Eng:

    def __init__(self, eng, selfc):
        self.e = eng
        self.ss = selfc
        self.last = None

    def wait(self, ctr, val):
        self.e.wait_ge(ctr.h, val)

    def chain(self):
        if self.last is not None:
            self.e.wait_ge(self.last[0].h, self.last[1])

    def run(self, name, *a, m=None, dma=False, **kw):
        self.chain()
        ins = getattr(self.e, name)(*a, **kw)
        if dma:
            m.inc(ins, 16)
        else:
            c = self.ss if m is None else m
            c.inc(ins, 1)
            self.last = (c, c.v)
        return ins


def build(debug=False):
    nc = bass.Bass("TRN2", target_bir_lowering=False)

    x_feat = nc.dram_tensor("x_feat", [128, ND, S], F32, kind="ExternalInput")
    x_tok = nc.dram_tensor("x_tok", [NT, 128, D], F32, kind="ExternalInput")
    wq_blob = nc.dram_tensor("wq_blob", [H, 128, ND, 128], F32, kind="ExternalInput")
    wk_blob = nc.dram_tensor("wk_blob", [KV, 128, ND, 128], F32, kind="ExternalInput")
    wv_rhs = nc.dram_tensor("wv_rhs", [ND, 128, KV * HD], F32, kind="ExternalInput")
    wo_t = nc.dram_tensor("wo_t", [32, 128, D], F32, kind="ExternalInput")
    a_blob = nc.dram_tensor("a_blob", [128, ND, 128], F32, kind="ExternalInput")
    b_pad = nc.dram_tensor("b_pad", [128, D], F32, kind="ExternalInput")
    cos_t = nc.dram_tensor("cos_t", [128, S], F32, kind="ExternalInput")
    sin_t = nc.dram_tensor("sin_t", [128, S], F32, kind="ExternalInput")
    keysc = nc.dram_tensor("keysc", [128, 17], F32, kind="ExternalInput")
    diagm = nc.dram_tensor("diagm", [128, 256], F32, kind="ExternalInput")
    i128 = nc.dram_tensor("i128", [128, 128], F32, kind="ExternalInput")
    e4 = nc.dram_tensor("e4", [4, 128], F32, kind="ExternalInput")
    onesf = nc.dram_tensor("onesf", [1, 128], F32, kind="ExternalInput")
    cvec = nc.dram_tensor("cvec", [128, 8], F32, kind="ExternalInput")
    ksraw = nc.dram_tensor("ksraw", [128, KV * 640], F32, kind="ExternalInput")
    out = nc.dram_tensor("out", [NT, 128, D], F32, kind="ExternalOutput")

    kind_stash = "ExternalOutput" if debug else "Internal"
    qstash = nc.dram_tensor("qstash", [H, 128, S], BF16, kind=kind_stash)
    kstash = nc.dram_tensor("kstash", [KV, 128, S], BF16, kind=kind_stash)
    vstash = nc.dram_tensor("vstash", [NT, 128, KV, HD], BF16, kind=kind_stash)
    gstash = nc.dram_tensor("gstash", [NT, 128, D], BF16, kind=kind_stash)
    ostash = nc.dram_tensor("ostash", [NT, 128, H, 128], BF16, kind=kind_stash)
    xdbg = None
    if debug:
        xdbg = nc.dram_tensor("xdbg", [2, 128, ND, HALF], BF16, kind="ExternalOutput")

    env = {"nc": nc, "debug": debug, "x_feat": x_feat, "x_tok": x_tok, "ksraw": ksraw,
           "cos_t": cos_t, "sin_t": sin_t,
           "wq_blob": wq_blob, "wk_blob": wk_blob, "wv_rhs": wv_rhs, "wo_t": wo_t,
           "out": out, "qstash": qstash, "kstash": kstash, "vstash": vstash,
           "gstash": gstash, "ostash": ostash, "xdbg": xdbg}

    with ExitStack() as ctx:
        def sb(name, shape, dt):
            t = ctx.enter_context(nc.sbuf_tensor(name, shape, dt))
            env[name] = t
            return t

        sb("keysc_sb", [128, 17], F32)
        sb("diag_sb", [128, 256], BF16)
        sb("i128_sb", [128, 128], F32)
        sb("e4_sb", [4, 128], BF16)
        sb("onesf_sb", [1, 128], F32)
        sb("ones_sb", [128, 128], BF16)
        sb("cvec_sb", [128, 8], F32)
        sb("bpad_sb", [128, D], BF16)
        sb("ab_sb", [128, ND, 128], BF16)

        def sem(name):
            c = Ctr(ctx.enter_context(nc.semaphore(name)))
            env[name] = c
            return c

        for n in ["ld_c", "init", "ld_x", "ld_x1", "ld_x2", "ld_x3", "vsq", "tsq", "stev", "vst", "sst",
                  "tbc", "vnorm", "tmid", "smid", "ld_lg", "tlt", "vga", "sge",
                  "vgt", "tgT", "sgT", "tgb", "vpg", "td", "ssil", "st_g0", "st_g1", "st_g2", "st_g3", "ld_w0", "ld_w1", "tpr", "sev", "ld_sw", "vrope",
                  "st_p0", "st_p1", "ld_v0", "ld_v1", "ld_v2", "ld_sw0", "ld_sw1", "tv", "svev", "st_v",
                  "ld_kv", "ld_ks", "vks", "ld_q0", "ld_q1", "tS", "sexp", "vdg", "tPV", "vON",
                  "st_o0", "st_o1", "st_o2", "st_o3", "sON0", "sON1", "vON0", "vON1", "ld_wo0", "ld_wo1", "ld_at0", "ld_at1", "ld_xg0",
                  "ld_xg1", "tW", "vP4", "st_u0", "st_u1", "st_dbg",
                  "sbc", "ld_k2", "ld_vv", "vself", "sself"]:
            sem(n)

        VE = Eng(nc.vector, env["vself"])
        SE = Eng(nc.scalar, env["sself"])
        env["VE"] = VE
        env["SE"] = SE

        ld_c, init = env["ld_c"], env["init"]

        env["const_pairs"] = [("keysc_sb", keysc), ("diag_sb", diagm),
                              ("i128_sb", i128), ("e4_sb", e4),
                              ("onesf_sb", onesf), ("cvec_sb", cvec),
                              ("bpad_sb", b_pad), ("ab_sb", a_blob)]
        with nc.Block() as blk:
            @blk.vector
            def _(v):
                VE.run("memset", env["ones_sb"][:, :], 1.0, m=init)

        e1 = ExitStack()
        with e1:
            def sb1(name, shape, dt):
                t = e1.enter_context(nc.sbuf_tensor(name, shape, dt))
                env[name] = t
                return t

            sb1("cos_sb", [128, S], F32)
            sb1("sin_sb", [128, S], F32)
            sb1("xn_sb", [128, ND, HALF], BF16)
            sb1("sq_sb", [128, 2, HALF], BF16)
            sb1("st_sb", [1, 3 * HALF], F32)
            sb1("wb_sb", [128, 2, ND, 128], BF16)
            sb1("wv_sb", [128, 3, 4, KV * HD], BF16)
            sb1("qe_sb", [128, 2, HALF], BF16)
            sb1("qs_sb", [128, 2, HALF], BF16)
            sb1("rt1_sb", [128, HALF], F32)
            sb1("rt2_sb", [128, HALF], F32)
            sb1("qr_sb", [128, 2, HALF], BF16)
            sb1("ve_sb", [128, 4, KV * HD], BF16)
            sb1("mid_sb", [128, HALF], F32)
            sb1("lg4_sb", [4, HALF], F32)
            sb1("gt_sb", [128, 32], F32)
            sb1("gTb_sb", [4, HALF], BF16)
            sb1("negm1_sb", [128, 8], F32)
            sb1("negm2_sb", [128, 8], F32)
            sb1("gtmp_sb", [128, 32], F32)
            sb1("keep_sb", [128, 32], F32)
            sb1("et_sb", [128, 32], F32)
            sb1("gs_sb", [128, 8], F32)
            sb1("grs_sb", [128, 8], F32)
            sb1("pg_sb", [128, HALF], BF16)
            sb1("gsil_sb", [128, 4, 1024], BF16)

            env["kpre_blob"] = wk_blob
            for half in range(2):
                with nc.Block() as blk0:
                    env["blk"] = blk0
                    _p0(half, env)
                    _ffn_mid(half, env)
                env.pop("blk", None)
                _ffn_gates_pre(half, env)
                _proj_rope(half, KV, wk_blob, kstash, env, light=True,
                           single=True)
                _ffn_gates_post(half, env)
                _preload_blobs(half, env, wq_blob, 2)
                _vproj(half, env)
                with nc.psum_tensor(f"ps_dl_{half}", [128, 2, 1024], F32) as ps_dl:
                    hk = _ffn_delta_hooks(half, env, ps_dl)
                    _proj_rope(half, H, wq_blob, qstash, env, light=False,
                               hooks=hk)

        with nc.Block() as blk:
            @blk.scalar
            def _(s):
                for n in ["st_p0", "st_p1", "st_v", "st_g0", "st_g1", "st_g2", "st_g3"]:
                    s.wait_ge(env[n].h, env[n].v)
                if debug:
                    s.wait_ge(env["st_dbg"].h, env["st_dbg"].v)
        nc.all_engine_barrier()

        env["wo_sb"] = ctx.enter_context(nc.sbuf_tensor("wo_sb", [128, 32, 1024], BF16))
        e2 = ExitStack()
        with e2:
            for name, shape, dt in [
                ("k_sb", [128, KV, S], BF16),
                ("kscal_sb", [128, KV, 640], BF16),
                ("v_sb", [128, NT, KV, HD], BF16),
                ("q_sb", [128, 2, 2, S], BF16),
                ("p_sb", [128, 8, 2, 512], BF16),
                ("rd_sb", [128, 2, 512], F32),
                ("o_sb", [128, 4, 512], BF16),
            ]:
                env[name] = e2.enter_context(nc.sbuf_tensor(name, shape, dt))
            _p3(env)

        with nc.Block() as blk:
            @blk.scalar
            def _(s):
                for n in ["st_o0", "st_o1", "st_o2", "st_o3"]:
                    s.wait_ge(env[n].h, env[n].v)
        nc.all_engine_barrier()

        e3 = ExitStack()
        with e3:
            for name, shape, dt in [
                ("wo2_sb", [128, 32, 1024], BF16),
                ("at_sb", [128, 2, H, 128], BF16),
                ("x4_sb", [128, 2, 1024], F32),
                ("g4_sb", [128, 2, 1024], BF16),
                ("o4_sb", [128, 2, 1024], F32),
                ("r4_sb", [128, 1024], F32),
            ]:
                env[name] = e3.enter_context(nc.sbuf_tensor(name, shape, dt))
            _p4(env)

    return nc


def _p0(half, env):
    nc = env["nc"]
    debug = env["debug"]
    VE, SE = env["VE"], env["SE"]
    (ld_x, vsq, tsq, stev, vst, sst, tbc, vnorm, ld_c, init, tpr, sev,
     st_dbg, sbc) = (env[n] for n in
                ["ld_x", "vsq", "tsq", "stev", "vst", "sst", "tbc", "vnorm",
                 "ld_c", "init", "tpr", "sev", "st_dbg", "sbc"])
    xn_sb, sq_sb, st_sb = (
        env[n] for n in ["xn_sb", "sq_sb", "st_sb"])
    tmp_sb = sq_sb
    ones_sb, onesf_sb, cvec_sb = env["ones_sb"], env["onesf_sb"], env["cvec_sb"]
    x_feat = env["x_feat"]
    t0 = half * HALF
    b_tsq = tsq.v
    b_vsq = vsq.v
    b_lx = [env[n].v for n in ["ld_x", "ld_x1", "ld_x2", "ld_x3"]]
    b_tpr = tpr.v
    b_sev = sev.v

    blk = env["blk"]
    with (
        nc.psum_tensor(f"ps_stat_{half}", [1, 2, HALF], F32) as ps_stat,
        nc.psum_tensor(f"ps_bc_{half}", [128, 2, HALF], F32) as ps_bc,
    ):
        @blk.gpsimd
        def _(g):
            if half == 1:
                g.wait_ge(tpr.h, b_tpr)
                g.wait_ge(env["tv"].h, env["tv"].v)
                g.wait_ge(env["tmid"].h, env["tmid"].v)
                g.wait_ge(tsq.h, b_tsq)
                g.wait_ge(vsq.h, b_vsq)
                if debug:
                    g.wait_ge(st_dbg.h, st_dbg.v)
            for q in range(4):
                lx = env[["ld_x", "ld_x1", "ld_x2", "ld_x3"][q]]
                g.dma_start(
                    xn_sb[:, 8 * q:8 * (q + 1), :],
                    x_feat[:, 8 * q:8 * (q + 1), t0:t0 + HALF],
                ).then_inc(lx.h, 16)
                lx.v += 16
            if half == 0:
                g.dma_start(env["cos_sb"][:, :], env["cos_t"][:, :]).then_inc(ld_c.h, 16)
                g.dma_start(env["sin_sb"][:, :], env["sin_t"][:, :]).then_inc(ld_c.h, 16)
                ld_c.v += 32
                for dn, srct in env["const_pairs"]:
                    dst = env[dn]
                    idx = tuple(slice(None) for _ in dst.shape)
                    sidx = tuple(slice(None) for _ in srct.shape)
                    g.dma_start(dst[idx], srct[sidx]).then_inc(ld_c.h, 16)
                    ld_c.v += 16
            kblob = env["kpre_blob"]
            env["qblob_base"] = [env["ld_w0"].v, env["ld_w1"].v]
            for f in range(2):
                lw = env["ld_w0"] if f % 2 == 0 else env["ld_w1"]
                g.dma_start(env["wb_sb"][:, f % 2, :, :],
                            kblob[f]).then_inc(lw.h, 16)
                lw.v += 16
            env["qblob_pre"] = 2
            wv_r = env["wv_rhs"].rearrange("c p f -> p c f")
            env["vblob_base"] = [env[n].v for n in ["ld_v0", "ld_v1", "ld_v2"]]
            for gi in range(3):
                lv = env[["ld_v0", "ld_v1", "ld_v2"][gi]]
                g.dma_start(env["wv_sb"][:, gi, :, :],
                            wv_r[:, 4 * gi:4 * (gi + 1), :]).then_inc(lv.h, 16)
                lv.v += 16
            env["vpre"] = 3

        @blk.vector
        def _(v):
            for c in range(ND):
                if c % 8 == 0:
                    lx = env[["ld_x", "ld_x1", "ld_x2", "ld_x3"][c // 8]]
                    VE.wait(lx, b_lx[c // 8] + 16)
                if c >= 2:
                    VE.wait(tsq, b_tsq + c - 1)
                VE.run("tensor_mul", sq_sb[:, c % 2, :], xn_sb[:, c, :],
                       xn_sb[:, c, :], m=vsq)

        @blk.tensor
        def _(t):
            t.wait_ge(init.h, 1)
            t.wait_ge(ld_c.h, ld_c.v)
            if half == 1:
                t.wait_ge(sev.h, b_sev)
            for c in range(ND):
                if c % 8 == 0:
                    lx = env[["ld_x", "ld_x1", "ld_x2", "ld_x3"][c // 8]]
                    t.wait_ge(lx.h, b_lx[c // 8] + 16)
                for b in range(2):
                    t.matmul(ps_stat[:, 0, 512 * b:512 * (b + 1)],
                             ones_sb[:, 0:1],
                             xn_sb[:, c, 512 * b:512 * (b + 1)],
                             start=(c == 0), stop=(c == ND - 1))
                t.wait_ge(vsq.h, b_vsq + c + 1)
                for b in range(2):
                    ins = t.matmul(ps_stat[:, 1, 512 * b:512 * (b + 1)],
                                   ones_sb[:, 0:1],
                                   sq_sb[:, c % 2, 512 * b:512 * (b + 1)],
                                   start=(c == 0), stop=(c == ND - 1))
                tsq.inc(ins)

        @blk.scalar
        def _(s):
            SE.wait(tsq, b_tsq + ND)
            SE.run("copy", st_sb[:, 0:HALF], ps_stat[:, 0, :])
            SE.run("copy", st_sb[:, HALF:2 * HALF], ps_stat[:, 1, :], m=stev)

        @blk.vector
        def _(v):
            VE.wait(stev, stev.v)
            VE.run("tensor_scalar_mul", st_sb[:, 0:HALF], st_sb[:, 0:HALF], 1.0 / D)
            VE.run("tensor_mul", st_sb[:, 2 * HALF:3 * HALF],
                   st_sb[:, 0:HALF], st_sb[:, 0:HALF])
            VE.run("tensor_scalar", st_sb[:, HALF:2 * HALF], st_sb[:, HALF:2 * HALF],
                   1.0 / D, EPS, OP.mult, OP.add)
            VE.run("tensor_sub", st_sb[:, 2 * HALF:3 * HALF],
                   st_sb[:, HALF:2 * HALF], st_sb[:, 2 * HALF:3 * HALF], m=vst)

        @blk.scalar
        def _(s):
            SE.wait(vst, vst.v)
            SE.run("activation", st_sb[:, 2 * HALF:3 * HALF],
                   st_sb[:, 2 * HALF:3 * HALF], AF.Ln)
            SE.run("activation", st_sb[:, HALF:2 * HALF],
                   st_sb[:, 2 * HALF:3 * HALF], AF.Exp,
                   scale=cvec_sb[0:1, CV_NEGHALF:CV_NEGHALF + 1], m=sst)

        @blk.tensor
        def _(t):
            t.wait_ge(sst.h, sst.v)
            for b in range(2):
                t.matmul(ps_bc[:, 0, 512 * b:512 * (b + 1)], onesf_sb[:, :],
                         st_sb[:, 512 * b:512 * (b + 1)],
                         start=True, stop=True)
                ins = t.matmul(ps_bc[:, 1, 512 * b:512 * (b + 1)], onesf_sb[:, :],
                               st_sb[:, HALF + 512 * b:HALF + 512 * (b + 1)],
                               start=True, stop=True)
            tbc.inc(ins)

        @blk.scalar
        def _(s):
            SE.wait(tbc, tbc.v)
            SE.run("copy", sq_sb[:, 0, :], ps_bc[:, 0, :])
            SE.run("copy", sq_sb[:, 1, :], ps_bc[:, 1, :], m=sbc)

        @blk.vector
        def _(v):
            VE.wait(sbc, sbc.v)
            for c in range(ND):
                VE.run("tensor_sub", xn_sb[:, c, :], xn_sb[:, c, :], sq_sb[:, 0, :])
                VE.run("tensor_mul", xn_sb[:, c, :], xn_sb[:, c, :], sq_sb[:, 1, :],
                       m=vnorm)

        if debug:
            @blk.scalar
            def _(s):
                SE.wait(vnorm, vnorm.v)
                SE.run("dma_start", env["xdbg"][half], xn_sb[:, :, :],
                       m=st_dbg, dma=True)


def _ffn_mid(half, env):
    nc = env["nc"]
    SE = env["SE"]
    vnorm, tmid, smid, ld_lg = (env[n] for n in ["vnorm", "tmid", "smid", "ld_lg"])
    xn_sb, ab_sb, mid_sb, lg4_sb = (env[n] for n in
                                    ["xn_sb", "ab_sb", "mid_sb", "lg4_sb"])

    blk = env["blk"]
    with (
        nc.psum_tensor(f"ps_mid_{half}", [128, HALF], F32) as ps_mid,
    ):
        @blk.tensor
        def _(t):
            b_vn = vnorm.v - ND
            for b in range(2):
                for c in range(ND):
                    if b == 0:
                        t.wait_ge(vnorm.h, b_vn + c + 1)
                    ins = t.matmul(ps_mid[:, 512 * b:512 * (b + 1)], ab_sb[:, c, :],
                                   xn_sb[:, c, 512 * b:512 * (b + 1)],
                                   start=(c == 0), stop=(c == ND - 1))
            tmid.inc(ins)

        @blk.scalar
        def _(s):
            SE.wait(tmid, tmid.v)
            SE.wait(env["ld_lg"], env["ld_lg"].v)
            SE.run("copy", mid_sb[:, :], ps_mid[:, :], m=smid)

        @blk.sync
        def _(sy):
            sy.wait_ge(smid.h, smid.v)
            sy.dma_start(lg4_sb[:, :], mid_sb[16:20, :]).then_inc(ld_lg.h, 16)
            ld_lg.v += 16


def _ffn_gates_pre(half, env):
    nc = env["nc"]
    VE, SE = env["VE"], env["SE"]
    (ld_lg, tlt, vga, sge, vgt) = (
        env[n] for n in ["ld_lg", "tlt", "vga", "sge", "vgt"])
    (lg4_sb, i128_sb, gt_sb, negm1_sb, negm2_sb, gtmp_sb, keep_sb, et_sb,
     gs_sb, grs_sb) = (
        env[n] for n in ["lg4_sb", "i128_sb", "gt_sb", "negm1_sb", "negm2_sb",
                         "gtmp_sb", "keep_sb", "et_sb", "gs_sb", "grs_sb"])

    with (
        nc.psum_tensor(f"ps_lt_{half}", [128, 32], F32) as ps_lt,
        nc.Block() as blk,
    ):
        @blk.tensor
        def _(t):
            t.wait_ge(ld_lg.h, ld_lg.v)
            for tt in range(8):
                ins = t.matmul(ps_lt[:, 4 * tt:4 * (tt + 1)],
                               lg4_sb[:, 128 * tt:128 * (tt + 1)],
                               i128_sb[0:4, 0:4], start=True, stop=True)
            tlt.inc(ins)

        @blk.vector
        def _(v):
            VE.wait(tlt, tlt.v)
            for tt in range(8):
                sl = ps_lt[:, 4 * tt:4 * (tt + 1)]
                VE.run("reduce_max", negm1_sb[:, tt:tt + 1], sl, AX.X, negate=True)
                VE.run("tensor_scalar", gtmp_sb[:, 4 * tt:4 * (tt + 1)], sl,
                       negm1_sb[:, tt:tt + 1], 0.0, OP.add, OP.is_ge)
                VE.run("scalar_tensor_tensor", gtmp_sb[:, 4 * tt:4 * (tt + 1)],
                       gtmp_sb[:, 4 * tt:4 * (tt + 1)], -1e30, sl,
                       OP.mult, OP.add)
                VE.run("reduce_max", negm2_sb[:, tt:tt + 1],
                       gtmp_sb[:, 4 * tt:4 * (tt + 1)], AX.X, negate=True)
                VE.run("tensor_scalar", keep_sb[:, 4 * tt:4 * (tt + 1)], sl,
                       negm2_sb[:, tt:tt + 1], 0.0, OP.add, OP.is_ge,
                       m=(vga if tt == 7 else None))

        @blk.scalar
        def _(s):
            SE.wait(vga, vga.v)
            for tt in range(8):
                SE.run("activation", et_sb[:, 4 * tt:4 * (tt + 1)],
                       ps_lt[:, 4 * tt:4 * (tt + 1)], AF.Exp,
                       bias=negm1_sb[:, tt:tt + 1],
                       m=(sge if tt == 7 else None))

        @blk.vector
        def _(v):
            VE.wait(sge, sge.v)
            VE.run("tensor_mul", et_sb[:, :], et_sb[:, :], keep_sb[:, :])
            for tt in range(8):
                VE.run("reduce_sum", gs_sb[:, tt:tt + 1],
                       et_sb[:, 4 * tt:4 * (tt + 1)], AX.X)
            VE.run("reciprocal", grs_sb[:, :], gs_sb[:, :])
            for tt in range(8):
                VE.run("tensor_scalar_mul", gt_sb[:, 4 * tt:4 * (tt + 1)],
                       et_sb[:, 4 * tt:4 * (tt + 1)], grs_sb[:, tt:tt + 1],
                       m=(vgt if tt == 7 else None))


def _ffn_gates_post(half, env):
    nc = env["nc"]
    VE, SE = env["VE"], env["SE"]
    (vnorm, vgt, tgT, sgT, tgb, vpg, sev) = (
        env[n] for n in ["vnorm", "vgt", "tgT", "sgT", "tgb", "vpg", "sev"])
    gt_sb, gTb_sb, i128_sb, e4_sb, pg_sb, mid_sb = (
        env[n] for n in ["gt_sb", "gTb_sb", "i128_sb", "e4_sb", "pg_sb",
                         "mid_sb"])

    with (
        nc.psum_tensor(f"ps_gT_{half}", [4, HALF], F32) as ps_gT,
        nc.Block() as blk,
    ):
        @blk.tensor
        def _(t):
            t.wait_ge(vgt.h, vgt.v)
            t.wait_ge(sev.h, sev.v)
            for tt in range(8):
                ins = t.matmul(ps_gT[:, 128 * tt:128 * (tt + 1)],
                               gt_sb[:, 4 * tt:4 * (tt + 1)], i128_sb[:, :],
                               start=True, stop=True)
            tgT.inc(ins)

        @blk.scalar
        def _(s):
            SE.wait(tgT, tgT.v)
            SE.run("copy", gTb_sb[:, :], ps_gT[:, :], m=sgT)

    with (
        nc.psum_tensor(f"ps_gb_{half}", [128, HALF], F32) as ps_gb,
        nc.Block() as blk,
    ):
        @blk.tensor
        def _(t):
            t.wait_ge(sgT.h, sgT.v)
            t.wait_ge(vnorm.h, vnorm.v)
            for b in range(2):
                ins = t.matmul(ps_gb[:, 512 * b:512 * (b + 1)], e4_sb[:, :],
                               gTb_sb[:, 512 * b:512 * (b + 1)], start=True, stop=True)
            tgb.inc(ins)

        @blk.vector
        def _(v):
            VE.wait(tgb, tgb.v)
            VE.run("tensor_mul", pg_sb[:, :], mid_sb[:, :], ps_gb[:, :], m=vpg)


def _ffn_delta_hooks(half, env, ps_dl):
    SE = env["SE"]
    (vpg, td, ssil, svev, sev) = (env[n] for n in
                                  ["vpg", "td", "ssil", "svev", "sev"])
    st_g = [env[f"st_g{i}"] for i in range(4)]
    pg_sb, gsil_sb, bpad_sb, cvec_sb = (
        env[n] for n in ["pg_sb", "gsil_sb", "bpad_sb", "cvec_sb"])
    gstash = env["gstash"]
    b_td = td.v
    b_sil = ssil.v
    b_g = [c.v for c in st_g]

    def t_hook(t, f):
        if f >= 32:
            return
        if f == 0:
            t.wait_ge(vpg.h, vpg.v)
            t.wait_ge(svev.h, svev.v)
            t.wait_ge(sev.h, sev.v)
        for k in (f,):
            tt, o2 = k // 4, k % 4
            if k >= 2:
                t.wait_ge(ssil.h, b_sil + k - 1)
            for b in range(2):
                ins = t.matmul(ps_dl[:, k % 2, 512 * b:512 * (b + 1)],
                               pg_sb[:, 128 * tt:128 * (tt + 1)],
                               bpad_sb[:, 1024 * o2 + 512 * b:
                                       1024 * o2 + 512 * (b + 1)],
                               start=True, stop=True)
            td.inc(ins)

    def s_hook(s, f):
        if f >= 32:
            return
        for k in (f,):
            tt, o2 = k // 4, k % 4
            stg = st_g[k % 4]
            SE.wait(td, b_td + k + 1)
            SE.wait(stg, b_g[k % 4] + 16 * (k // 4))
            SE.run("activation", gsil_sb[:, k % 4, :], ps_dl[:, k % 2, :],
                   AF.Silu, scale=cvec_sb[:, CV_FOUR:CV_FOUR + 1], m=ssil)
            SE.run("dma_start",
                   gstash[half * 8 + tt][:, 1024 * o2:1024 * (o2 + 1)],
                   gsil_sb[:, k % 4, :], m=stg, dma=True)

    return {"tensor": t_hook, "scalar": s_hook}


def _proj_rope(half, NF, blob, stash, env, light=False, single=False, hooks=None):
    nc = env["nc"]
    VE, SE = env["VE"], env["SE"]
    (ld_w0, ld_w1, tpr, sev, ld_sw0, ld_sw1, vrope, st_p0, st_p1, vnorm,
     ld_c) = (
        env[n] for n in ["ld_w0", "ld_w1", "tpr", "sev", "ld_sw0", "ld_sw1",
                         "vrope", "st_p0", "st_p1", "vnorm", "ld_c"])
    (wb_sb, qe_sb, qs_sb, rt1_sb, rt2_sb, qr_sb, xn_sb, cos_sb, sin_sb) = (
        env[n] for n in ["wb_sb", "qe_sb", "qs_sb", "rt1_sb", "rt2_sb",
                         "qr_sb", "xn_sb", "cos_sb", "sin_sb"])
    t0 = half * HALF
    pre = env.pop("qblob_pre", 0)
    if pre:
        b_w = env.pop("qblob_base")
    else:
        b_w = [ld_w0.v, ld_w1.v]
    b_tpr = tpr.v
    b_sev = sev.v
    b_sw = [ld_sw0.v, ld_sw1.v]
    b_rp = vrope.v
    b_p = [st_p0.v, st_p1.v]

    es_p = ExitStack()
    with es_p, nc.Block() as blk:
        ps_pa = es_p.enter_context(
            nc.psum_tensor(f"ps_pa_{half}_{NF}", [128, HALF], F32))
        if single:
            ps_p = [ps_pa, ps_pa]
        else:
            ps_pb = es_p.enter_context(
                nc.psum_tensor(f"ps_pb_{half}_{NF}", [128, HALF], F32))
            ps_p = [ps_pa, ps_pb]
        @blk.gpsimd
        def _(g):
            def load_blob(f):
                par = f % 2
                lw = ld_w0 if par == 0 else ld_w1
                g.wait_ge(tpr.h, b_tpr + max(0, f - 1))
                g.dma_start(wb_sb[:, par, :, :], blob[f]).then_inc(lw.h, 16)
                lw.v += 16

            for f in range(pre, NF):
                load_blob(f)

        @blk.sync
        def _(sy):
            for f in range(NF):
                par = f % 2
                lsw = ld_sw0 if par == 0 else ld_sw1
                sy.wait_ge(sev.h, b_sev + f + 1)
                sy.wait_ge(vrope.h, b_rp + max(0, f - 1))
                sy.dma_start(qs_sb[0:64, par, :],
                             qe_sb[64:128, par, :]).then_inc(lsw.h, 16)
                sy.dma_start(qs_sb[64:128, par, :],
                             qe_sb[0:64, par, :]).then_inc(lsw.h, 16)
                lsw.v += 32

        @blk.tensor
        def _(t):
            t.wait_ge(vnorm.h, vnorm.v)
            if light:
                t.wait_ge(env["smid"].h, env["smid"].v)
            else:
                t.wait_ge(env["vpg"].h, env["vpg"].v)
            t.wait_ge(env["svev"].h, env["svev"].v)
            t.wait_ge(sev.h, b_sev)
            for f in range(NF):
                par = f % 2
                lw = ld_w0 if par == 0 else ld_w1
                t.wait_ge(lw.h, b_w[par] + 16 * (f // 2 + 1))
                if single:
                    if f >= 1:
                        t.wait_ge(sev.h, b_sev + f)
                elif f >= 2:
                    t.wait_ge(sev.h, b_sev + f - 1)
                for c in range(ND):
                    for b in range(2):
                        ins = t.matmul(ps_p[0 if single else par][:, 512 * b:512 * (b + 1)],
                                       wb_sb[:, par, c, :],
                                       xn_sb[:, c, 512 * b:512 * (b + 1)],
                                       start=(c == 0), stop=(c == ND - 1))
                tpr.inc(ins)
                if hooks is not None:
                    hooks["tensor"](t, f)

        @blk.scalar
        def _(s):
            for f in range(NF + 1):
                if f < NF:
                    par = f % 2
                    SE.wait(tpr, b_tpr + f + 1)
                    lsw = ld_sw0 if par == 0 else ld_sw1
                    SE.wait(lsw, b_sw[par] + 32 * (f // 2))
                    SE.wait(vrope, b_rp + max(0, f - 1))
                    SE.run("copy", qe_sb[:, par, :], ps_p[0 if single else par][:, :], m=sev)
                if f >= 1:
                    fp = f - 1
                    par = fp % 2
                    stp = st_p0 if par == 0 else st_p1
                    SE.wait(vrope, b_rp + fp + 1)
                    SE.run("dma_start", stash[fp][:, t0:t0 + HALF],
                           qr_sb[:, par, :], m=stp, dma=True)
                if hooks is not None and f < NF:
                    hooks["scalar"](s, f)

        @blk.vector
        def _(v):
            VE.wait(ld_c, ld_c.v)
            for f in range(NF):
                par = f % 2
                stp = st_p0 if par == 0 else st_p1
                lsw = ld_sw0 if par == 0 else ld_sw1
                VE.wait(lsw, b_sw[par] + 32 * (f // 2 + 1))
                VE.wait(stp, b_p[par] + 16 * (f // 2))
                VE.run("tensor_mul", rt1_sb[:, :], qe_sb[:, par, :],
                       cos_sb[:, t0:t0 + HALF])
                VE.run("tensor_mul", rt2_sb[:, :], qs_sb[:, par, :],
                       sin_sb[:, t0:t0 + HALF])
                VE.run("tensor_add", qr_sb[:, par, :], rt1_sb[:, :], rt2_sb[:, :],
                       m=vrope)


def _preload_blobs(half, env, blob, n):
    nc = env["nc"]
    ld_w0, ld_w1, tpr = env["ld_w0"], env["ld_w1"], env["tpr"]
    env["qblob_base"] = [ld_w0.v, ld_w1.v]
    wb_sb = env["wb_sb"]
    with nc.Block() as blk:
        @blk.gpsimd
        def _(g):
            for f in range(n):
                par = f % 2
                lw = ld_w0 if par == 0 else ld_w1
                g.wait_ge(tpr.h, tpr.v)
                g.dma_start(wb_sb[:, par, :, :], blob[f]).then_inc(lw.h, 16)
                lw.v += 16
    env["qblob_pre"] = n


def _vproj(half, env):
    nc = env["nc"]
    SE = env["SE"]
    ld_v0, ld_v1, ld_v2, tv, svev, st_v, vnorm = (
        env[n] for n in ["ld_v0", "ld_v1", "ld_v2", "tv", "svev", "st_v",
                         "vnorm"])
    lvs = [ld_v0, ld_v1, ld_v2]
    wv_sb, ve_sb, xn_sb = env["wv_sb"], env["ve_sb"], env["xn_sb"]
    wv_rhs, vstash = env["wv_rhs"], env["vstash"]
    pre_v = env.pop("vpre", 0)
    if pre_v:
        b_v = env.pop("vblob_base")
    else:
        b_v = [ld_v0.v, ld_v1.v, ld_v2.v]
    b_tv = tv.v
    b_sv = svev.v
    b_stv = st_v.v

    with (
        nc.psum_tensor(f"ps_v_{half}", [128, 4, KV * HD], F32) as ps_v,
        nc.Block() as blk,
    ):
        @blk.gpsimd
        def _(g):
            for gi in range(pre_v, 2 * ND // 4):
                par = gi % 3
                lv = lvs[par]
                cc = (gi * 4) % ND
                g.wait_ge(tv.h, b_tv + max(0, 4 * (gi - 2)))
                g.dma_start(wv_sb[:, par, :, :],
                            wv_rhs.rearrange("c p f -> p c f")[:, cc:cc + 4, :]
                            ).then_inc(lv.h, 16)
                lv.v += 16

        @blk.tensor
        def _(t):
            t.wait_ge(vnorm.h, vnorm.v)
            t.wait_ge(env["vpg"].h, env["vpg"].v)
            t.wait_ge(env["sev"].h, env["sev"].v)
            t.wait_ge(svev.h, b_sv)
            for grp in range(2):
                if grp == 1:
                    t.wait_ge(svev.h, b_sv + 4)
                for c in range(ND):
                    k = grp * ND + c
                    gi = k // 4
                    par = gi % 3
                    lv = lvs[par]
                    t.wait_ge(lv.h, b_v[par] + 16 * (gi // 3 + 1))
                    for tl in range(4):
                        tt = grp * 4 + tl
                        for b in range(2):
                            ins = t.matmul(ps_v[:, tl, 512 * b:512 * (b + 1)],
                                           xn_sb[:, c, 128 * tt:128 * (tt + 1)],
                                           wv_sb[:, par, c % 4, 512 * b:512 * (b + 1)],
                                           start=(c == 0), stop=(c == ND - 1))
                    tv.inc(ins)

        @blk.scalar
        def _(s):
            for grp in range(2):
                SE.wait(tv, b_tv + ND * (grp + 1))
                SE.wait(st_v, b_stv + (64 if grp == 1 else 0))
                for tl in range(4):
                    tt = grp * 4 + tl
                    SE.run("copy", ve_sb[:, tl, :], ps_v[:, tl, :], m=svev)
                    SE.run("dma_start", vstash[half * 8 + tt], ve_sb[:, tl, :],
                           m=st_v, dma=True)


def _p3(env):
    nc = env["nc"]
    VE, SE = env["VE"], env["SE"]
    (ld_kv, ld_q0, ld_q1, tS, sexp, vdg, tPV, ld_wo0, ld_c, init) = (
        env[n] for n in ["ld_kv", "ld_q0", "ld_q1", "tS", "sexp", "vdg", "tPV",
                         "ld_wo0", "ld_c", "init"])
    sON = [env["sON0"], env["sON1"]]
    vON = [env["vON0"], env["vON1"]]
    st_o = [env["st_o0"], env["st_o1"], env["st_o2"], env["st_o3"]]
    k_sb, kscal_sb, v_sb, q_sb, p_sb, rd_sb, o_sb = (
        env[n] for n in ["k_sb", "kscal_sb", "v_sb", "q_sb", "p_sb", "rd_sb",
                         "o_sb"])
    keysc_sb, diag_sb, ones_sb, wo_sb, cvec_sb = (
        env[n] for n in ["keysc_sb", "diag_sb", "ones_sb", "wo_sb", "cvec_sb"])
    kstash, vstash, qstash, ostash, wo_t = (
        env[n] for n in ["kstash", "vstash", "qstash", "ostash", "wo_t"])
    ksraw_r = env["ksraw"]

    flat = []
    pb = 0
    for (pr, qt, bands) in PAIR_SCHED:
        lst = []
        for (j, lo, hi) in bands:
            lst.append((pb, j, lo, hi))
            pb += 1
        flat.append((pr, qt, lst))
    n_band = pb
    dplan = {}
    dc = 0
    for (pr, qt, lst) in flat:
        for (g, j, lo, hi) in lst:
            if j >= 4 * qt:
                dc += 1
            dplan[g] = dc

    pv_order = []
    for (pr, qt, lst) in flat:
        seg = [(e, pr, qt) for e in lst]
        pv_order += [seg[1], seg[2], seg[0]] + seg[3:]
    env["pv_order"] = pv_order
    pvdone = {}
    for pos, (e, pr, qt) in enumerate(pv_order):
        pvdone[e[0]] = 2 * (pos + 1)
    gslot = {}
    for si, (pr, qt, lst) in enumerate(flat):
        for (g, j, lo, hi) in lst:
            gslot[g] = (6 + si % 2) if j == 0 else g % 6
    env["gslot"] = gslot
    slot_wait = {}
    last_user = {}
    for (pr, qt, lst) in flat:
        for (g, j, lo, hi) in lst:
            slot = gslot[g]
            if slot in last_user:
                slot_wait[g] = pvdone[last_user[slot]]
            last_user[slot] = g
    env["slot_wait"] = slot_wait

    with (
        nc.psum_tensor("ps_S", [128, 2, 2, 512], F32) as ps_S,
        nc.psum_tensor("ps_o", [128, 2, 512], F32) as ps_o,
        nc.psum_tensor("ps_d3", [128, 2, 512], F32) as ps_d3,
        nc.Block() as blk,
    ):
        @blk.gpsimd
        def _(g):
            g.dma_start(kscal_sb[:, :, :].rearrange("p k t -> p (k t)"),
                        ksraw_r[:, :]).then_inc(env["ld_ks"].h, 16)
            env["ld_ks"].v += 16
            for hc in range(32):
                g.dma_start(wo_sb[:, hc, :],
                            wo_t[hc][:, 0:1024]).then_inc(ld_wo0.h, 16)
                ld_wo0.v += 16

        @blk.sync
        def _(sy):
            sy.dma_start(k_sb[:, 0:4, :],
                         kstash.rearrange("f p t -> p f t")[:, 0:4, :]
                         ).then_inc(ld_kv.h, 16)
            ld_kv.v += 16
            for pr in range(H // 2):
                lq = ld_q0 if pr % 2 == 0 else ld_q1
                if pr >= 2:
                    sy.wait_ge(tS.h, BANDS_PER_PAIR * (pr - 1))
                for X in range(2):
                    sy.dma_start(q_sb[:, pr % 2, X, :],
                                 qstash[2 * pr + X]).then_inc(lq.h, 16)
                    lq.v += 16
                if pr == 1:
                    sy.dma_start(k_sb[:, 4:8, :],
                                 kstash.rearrange("f p t -> p f t")[:, 4:8, :]
                                 ).then_inc(env["ld_k2"].h, 16)
                    env["ld_k2"].v += 16
                if pr >= 1:
                    for qt in range(4):
                        _ostash_write(sy, env, pr - 1, qt, vON, st_o, o_sb, ostash)
            for qt in range(4):
                _ostash_write(sy, env, H // 2 - 1, qt, vON, st_o, o_sb, ostash)

        @blk.tensor
        def _(t):
            t.wait_ge(init.h, 1)

            def emit_S(g, j, lo, hi, pr, qt):
                kvh = pr // 2
                L = hi - lo
                sp = g % 2
                if g >= 2:
                    t.wait_ge(sexp.h, g - 1)
                for X in range(2):
                    if qt == 1 and j <= 4:
                        bnd = BOUND - lo
                        t.matmul(ps_S[:, sp, X, 0:bnd],
                                 k_sb[:, kvh, 128 * j:128 * (j + 1)],
                                 q_sb[:, pr % 2, X, lo:lo + bnd],
                                 start=True, stop=False)
                        ins = t.matmul(ps_S[:, sp, X, bnd:L],
                                       kscal_sb[:, kvh, 128 * j:128 * (j + 1)],
                                       q_sb[:, pr % 2, X, lo + bnd:hi],
                                       start=False, stop=True)
                    else:
                        kk = kscal_sb if (qt >= 2 and j <= 4) else k_sb
                        ins = t.matmul(ps_S[:, sp, X, 0:L],
                                       kk[:, kvh, 128 * j:128 * (j + 1)],
                                       q_sb[:, pr % 2, X, lo:lo + L],
                                       start=True, stop=True)
                tS.inc(ins)

            def emit_PV(g, j, lo, hi, pr, qt):
                kvh = pr // 2
                L = hi - lo
                pp = env["gslot"][g]
                jlast = 4 * qt + 3
                if j >= 4 * qt:
                    t.wait_ge(vdg.h, dplan[g])
                else:
                    t.wait_ge(sexp.h, g + 1)
                segidx = 4 * pr + qt
                q0 = lo - 512 * qt
                first = (j == 1)
                if j == 0:
                    parts = [(0, 128), (128, 512)]
                else:
                    parts = [(q0, 512)]
                for X in range(2):
                    if first and segidx >= 1:
                        t.wait_ge(vON[X].h, segidx)
                    for (z0, z1) in parts:
                        t.matmul(ps_o[:, X, z0:z1], v_sb[:, j, kvh, :],
                                 p_sb[:, pp, X, z0 - q0:z1 - q0],
                                 start=first, stop=(j == jlast))
                        ins = t.matmul(ps_d3[:, X, z0:z1], ones_sb[:, :],
                                       p_sb[:, pp, X, z0 - q0:z1 - q0],
                                       start=first, stop=(j == jlast))
                    tPV.inc(ins)

            LAG = 4
            allb = []
            pv_order = env["pv_order"]
            for (pr, qt, lst) in flat:
                for e in lst:
                    allb.append((e, pr, qt))
            v_waited = [False]
            for i, (e, pr, qt) in enumerate(allb):
                if qt == 0 and e[1] == 0:
                    lq = ld_q0 if pr % 2 == 0 else ld_q1
                    t.wait_ge(lq.h, 32 * (pr // 2 + 1))
                    t.wait_ge(env["vks"].h, pr // 2 + 1)
                emit_S(*e, pr, qt)
                if i >= LAG:
                    if not v_waited[0]:
                        t.wait_ge(env["ld_vv"].h, 16)
                        v_waited[0] = True
                    (e2, pr2, qt2) = pv_order[i - LAG]
                    emit_PV(*e2, pr2, qt2)
            for i in range(len(allb) - LAG, len(allb)):
                (e2, pr2, qt2) = pv_order[i]
                emit_PV(*e2, pr2, qt2)

        @blk.scalar
        def _(s):
            SE.run("dma_start", v_sb[:, :, :, :],
                   vstash.rearrange("t p k h -> p t k h"),
                   m=env["ld_vv"], dma=True)
            SE.wait(ld_c, ld_c.v)
            pend = None

            def emit_rden(last_pb):
                SE.wait(tPV, 2 * (last_pb + 1))
                SE.run("activation", rd_sb[:, :, :], ps_d3[:, :, :], AF.Ln)
                SE.run("activation", rd_sb[:, :, :], rd_sb[:, :, :], AF.Exp,
                       scale=cvec_sb[:, CV_NEG1:CV_NEG1 + 1], m=env["sON0"])

            for (pr, qt, lst) in flat:
                for bi, (g, j, lo, hi) in enumerate(lst):
                    L = hi - lo
                    sp = g % 2
                    SE.wait(tS, g + 1)
                    if g in env["slot_wait"]:
                        SE.wait(tPV, env["slot_wait"][g])
                    SE.run("activation",
                           p_sb[:, env["gslot"][g], :, 0:L],
                           ps_S[:, sp, :, 0:L],
                           AF.Exp, scale=keysc_sb[:, 16:17], m=sexp)
                    if bi == 1 and pend is not None:
                        emit_rden(pend)
                        pend = None
                pend = lst[-1][0]
            emit_rden(pend)
        @blk.vector
        def _(v):
            VE.wait(ld_c, ld_c.v)
            VE.wait(ld_kv, 16)
            VE.wait(env["ld_ks"], 16)
            for kv in range(KV):
                if kv == 4:
                    VE.wait(env["ld_k2"], 16)
                VE.run("tensor_mul", kscal_sb[:, kv, :], k_sb[:, kv, 0:640],
                       kscal_sb[:, kv, :], m=env["vks"])
            for (pr, qt, lst) in flat:
                segidx = 4 * pr + qt
                for (g, j, lo, hi) in lst:
                    if j >= 4 * qt:
                        VE.wait(sexp, g + 1)
                        pp = env["gslot"][g]
                        VE.run("tensor_mul", p_sb[:, pp, :, 0:128],
                               p_sb[:, pp, :, 0:128],
                               diag_sb[:, :].rearrange("p (a x) -> p a x", a=2),
                               m=vdg)
                for X in range(2):
                    sgl = 2 * segidx + X
                    VE.wait(env["sON0"], segidx + 1)
                    VE.wait(st_o[sgl % 4], 16 * (sgl // 4))
                    VE.run("tensor_mul", o_sb[:, sgl % 4, :], ps_o[:, X, :],
                           rd_sb[:, X, :], m=vON[X])


def _ostash_write(sy, env, pr, qt, vON, st_o, o_sb, ostash):
    segidx = 4 * pr + qt
    for X in range(2):
        sgl = 2 * segidx + X
        sy.wait_ge(vON[X].h, segidx + 1)
        sto = st_o[sgl % 4]
        sy.dma_start(
            ostash.rearrange("t p h x -> p t h x")[:, 4 * qt:4 * qt + 4,
                                                   2 * pr + X, :],
            o_sb[:, sgl % 4, :].rearrange("p (t x) -> p t x", t=4),
        ).then_inc(sto.h, 16)
        sto.v += 16


def _p4(env):
    nc = env["nc"]
    VE, SE = env["VE"], env["SE"]
    (ld_wo0, ld_wo1, ld_at0, ld_at1, ld_xg0, ld_xg1, tW, vP4, st_u0,
     st_u1) = (env[n] for n in
               ["ld_wo0", "ld_wo1", "ld_at0", "ld_at1", "ld_xg0",
                "ld_xg1", "tW", "vP4", "st_u0", "st_u1"])
    wo_sb, wo2_sb, at_sb, x4_sb, g4_sb, o4_sb, r4_sb = (
        env[n] for n in ["wo_sb", "wo2_sb", "at_sb", "x4_sb", "g4_sb",
                         "o4_sb", "r4_sb"])
    wo_t, ostash, x_tok, gstash, out = (
        env[n] for n in ["wo_t", "ostash", "x_tok", "gstash", "out"])

    with (
        nc.psum_tensor("ps_w", [128, 2, 1024], F32) as ps_w,
        nc.Block() as blk,
    ):
        @blk.gpsimd
        def _(g):
            for q in range(1, 4):
                buf = wo_sb if q % 2 == 0 else wo2_sb
                lwo = ld_wo0 if q % 2 == 0 else ld_wo1
                g.wait_ge(tW.h, 16 * (q - 1))
                for hc in range(32):
                    g.dma_start(buf[:, hc, :],
                                wo_t[hc][:, 1024 * q:1024 * (q + 1)]
                                ).then_inc(lwo.h, 16)
                    lwo.v += 16

        @blk.sync
        def _(sy):
            for i in range(4):
                sy.wait_ge(env[f"st_g{i}"].h, env[f"st_g{i}"].v)
            k = 0
            for q in range(4):
                for tt in range(NT):
                    par = k % 2
                    lat = ld_at0 if par == 0 else ld_at1
                    lxg = ld_xg0 if par == 0 else ld_xg1
                    if k >= 2:
                        sy.wait_ge(tW.h, k - 1)
                        sy.wait_ge(vP4.h, k - 1)
                    sy.dma_start(at_sb[:, par, :, :], ostash[tt]).then_inc(lat.h, 16)
                    sy.dma_start(x4_sb[:, par, :],
                                 x_tok[tt][:, 1024 * q:1024 * (q + 1)]
                                 ).then_inc(lxg.h, 16)
                    sy.dma_start(g4_sb[:, par, :],
                                 gstash[tt][:, 1024 * q:1024 * (q + 1)]
                                 ).then_inc(lxg.h, 16)
                    lat.v += 16
                    lxg.v += 32
                    k += 1

        @blk.tensor
        def _(t):
            k = 0
            for q in range(4):
                lwo = ld_wo0 if q % 2 == 0 else ld_wo1
                t.wait_ge(lwo.h, 512 * (q // 2 + 1))
                buf = wo_sb if q % 2 == 0 else wo2_sb
                for tt in range(NT):
                    par = k % 2
                    lat = ld_at0 if par == 0 else ld_at1
                    t.wait_ge(lat.h, 16 * (k // 2 + 1))
                    if k >= 2:
                        t.wait_ge(vP4.h, k - 1)
                    for hc in range(32):
                        for b in range(2):
                            ins = t.matmul(ps_w[:, par, 512 * b:512 * (b + 1)],
                                           at_sb[:, par, hc, :],
                                           buf[:, hc, 512 * b:512 * (b + 1)],
                                           start=(hc == 0), stop=(hc == 31))
                    tW.inc(ins)
                    k += 1

        @blk.vector
        def _(v):
            k = 0
            for q in range(4):
                for tt in range(NT):
                    par = k % 2
                    lxg = ld_xg0 if par == 0 else ld_xg1
                    stu = st_u0 if par == 0 else st_u1
                    VE.wait(tW, k + 1)
                    VE.wait(lxg, 32 * (k // 2 + 1))
                    if k >= 2:
                        VE.wait(stu, 16 * (k // 2))
                    VE.run("tensor_mul", r4_sb[:, :], ps_w[:, par, :],
                           g4_sb[:, par, :])
                    VE.run("tensor_add", o4_sb[:, par, :], r4_sb[:, :],
                           x4_sb[:, par, :], m=vP4)
                    k += 1

        @blk.scalar
        def _(s):
            k = 0
            for q in range(4):
                for tt in range(NT):
                    par = k % 2
                    stu = st_u0 if par == 0 else st_u1
                    SE.wait(vP4, k + 1)
                    SE.run("dma_start", out[tt][:, 1024 * q:1024 * (q + 1)],
                           o4_sb[:, par, :], m=stu, dma=True)
                    k += 1
            s.wait_ge(st_u0.h, st_u0.v)
            s.wait_ge(st_u1.h, st_u1.v)

        @blk.sync
        def _(sy):
            sy.wait_ge(st_u0.h, st_u0.v)
            sy.wait_ge(st_u1.h, st_u1.v)



def _prep_shared(wq, wk, wv, wo, router_w, A, B):
    f32 = np.float32
    wq_blob = np.ascontiguousarray(wq.reshape(H, 128, ND, 128).transpose(0, 3, 2, 1))
    wk_blob = np.ascontiguousarray(wk.reshape(KV, 128, ND, 128).transpose(0, 3, 2, 1))
    wv_rhs = np.ascontiguousarray(wv.T.reshape(ND, 128, KV * HD))
    wo_tm = np.ascontiguousarray(wo.T).reshape(32, 128, D)

    Apad = np.zeros((D, 128), dtype=f32)
    for e in range(E):
        Apad[:, 32 * e:32 * e + 16] = A[e].T
    Apad[:, 16:20] = router_w.T
    a_blobm = np.ascontiguousarray(Apad.reshape(ND, 128, 128).transpose(1, 0, 2))

    Bpad = np.zeros((128, D), dtype=f32)
    for e in range(E):
        Bpad[32 * e:32 * e + 16, :] = B[e].T

    inv = 1.0 / 10000.0 ** (np.arange(0, HD, 2, dtype=np.float64) / HD)
    pos = np.arange(S, dtype=np.float64)
    fr = pos[:, None] * inv[None, :]
    cos64 = np.cos(fr)
    sin64 = np.sin(fr)
    cos_full = np.concatenate([cos64, cos64], axis=1)
    sinS = np.concatenate([-sin64, sin64], axis=1)
    cos_T = np.ascontiguousarray(cos_full.T).astype(f32)
    sin_T = np.ascontiguousarray(sinS.T).astype(f32)

    kpos = np.arange(S)
    key_scale = np.where(kpos < SYS, 0.5, np.where(kpos < BOUND, 1.5, 1.0))
    ks = np.zeros((128, 17), dtype=f32)
    for j in range(16):
        ks[:, j] = C_INV * key_scale[128 * j:128 * (j + 1)]
    ks[:, 16] = C_INV
    ksr = np.broadcast_to(np.tile(key_scale[:640], KV)[None, :],
                          (128, KV * 640)).astype(f32)
    ksr = np.ascontiguousarray(ksr)

    dia = np.zeros((128, 256), dtype=f32)
    for a in range(128):
        dia[a, a:128] = 1.0
        dia[a, 128 + a:] = 1.0

    i128m = np.eye(128, dtype=f32)
    e4m = np.zeros((4, 128), dtype=f32)
    for e in range(E):
        e4m[e, 32 * e:32 * e + 16] = 1.0
    onesm = np.ones((1, 128), dtype=f32)
    cv = np.zeros((128, 8), dtype=f32)
    cv[:, CV_NEGHALF] = -0.5
    cv[:, CV_NEG1] = -1.0
    cv[:, CV_FOUR] = 4.0

    return dict(wq_blob=wq_blob, wk_blob=wk_blob, wv_rhs=wv_rhs, wo_t=wo_tm,
                a_blob=a_blobm, b_pad=Bpad, cos_t=cos_T, sin_t=sin_T, keysc=ks,
                diagm=dia, i128=i128m, e4=e4m, onesf=onesm, cvec=cv, ksraw=ksr)


def _prep_core(x2d):
    x_tok = np.ascontiguousarray(x2d).reshape(NT, 128, D)
    x_feat = np.ascontiguousarray(x2d.T.reshape(ND, 128, S).transpose(1, 0, 2))
    return dict(x_tok=x_tok, x_feat=x_feat)


def kernel(L0_attn_hidden_states, L0_attn_wq, L0_attn_wk, L0_attn_wv, L0_attn_wo,
           L0_ffn_router_w, L0_ffn_A, L0_ffn_B):
    x = np.asarray(L0_attn_hidden_states, dtype=np.float32)
    shared = _prep_shared(
        np.asarray(L0_attn_wq, dtype=np.float32),
        np.asarray(L0_attn_wk, dtype=np.float32),
        np.asarray(L0_attn_wv, dtype=np.float32),
        np.asarray(L0_attn_wo, dtype=np.float32),
        np.asarray(L0_ffn_router_w, dtype=np.float32),
        np.asarray(L0_ffn_A, dtype=np.float32),
        np.asarray(L0_ffn_B, dtype=np.float32))

    nc = build()
    in_maps = []
    for b in range(8):
        m = dict(shared)
        m.update(_prep_core(x[b]))
        in_maps.append(m)
    res = run_bass_kernel_spmd(nc, in_maps, core_ids=list(range(8)))
    outs = [np.asarray(r["out"]).reshape(S, D) for r in res.results]
    return np.stack(outs, axis=0)


# version 26
# speedup vs baseline: 1.0430x; 1.0430x over previous
import math
from contextlib import ExitStack

import numpy as np
import concourse.bass as bass
import concourse.mybir as mybir
from concourse.bass_utils import run_bass_kernel_spmd

F32 = mybir.dt.float32
BF16 = mybir.dt.bfloat16
AF = mybir.ActivationFunctionType
OP = mybir.AluOpType
AX = mybir.AxisListType

S = 2048
D = 4096
H = 32
KV = 8
HD = 128
ND = 32
NT = 16
HALF = 1024
E = 4
SYS = 35
IMG = 576
BOUND = SYS + IMG
C_INV = 1.0 / math.sqrt(HD)
EPS = 1e-5

CV_NEGHALF = 0
CV_NEG1 = 1
CV_FOUR = 2

PAIR_SCHED = []
for _pr in range(H // 2):
    for _qt in range(4):
        _bands = [(j, max(512 * _qt, 128 * j), 512 * (_qt + 1))
                  for j in range(4 * _qt + 4)]
        PAIR_SCHED.append((_pr, _qt, _bands))
BANDS_PER_PAIR = 4 + 8 + 12 + 16


class Ctr:

    def __init__(self, h):
        self.h = h
        self.v = 0

    def inc(self, instr, k=1):
        instr.then_inc(self.h, k)
        self.v += k
        return self.v


class Eng:

    def __init__(self, eng, selfc):
        self.e = eng
        self.ss = selfc
        self.last = None

    def wait(self, ctr, val):
        self.e.wait_ge(ctr.h, val)

    def chain(self):
        if self.last is not None:
            self.e.wait_ge(self.last[0].h, self.last[1])

    def run(self, name, *a, m=None, dma=False, **kw):
        self.chain()
        ins = getattr(self.e, name)(*a, **kw)
        if dma:
            m.inc(ins, 16)
        else:
            c = self.ss if m is None else m
            c.inc(ins, 1)
            self.last = (c, c.v)
        return ins


def build(debug=False):
    nc = bass.Bass("TRN2", target_bir_lowering=False)

    x_feat = nc.dram_tensor("x_feat", [128, ND, S], F32, kind="ExternalInput")
    x_tok = nc.dram_tensor("x_tok", [NT, 128, D], F32, kind="ExternalInput")
    wq_blob = nc.dram_tensor("wq_blob", [H, 128, ND, 128], F32, kind="ExternalInput")
    wk_blob = nc.dram_tensor("wk_blob", [KV, 128, ND, 128], F32, kind="ExternalInput")
    wv_rhs = nc.dram_tensor("wv_rhs", [ND, 128, KV * HD], F32, kind="ExternalInput")
    wo_t = nc.dram_tensor("wo_t", [32, 128, D], F32, kind="ExternalInput")
    a_blob = nc.dram_tensor("a_blob", [128, ND, 128], F32, kind="ExternalInput")
    b_pad = nc.dram_tensor("b_pad", [128, D], F32, kind="ExternalInput")
    cos_t = nc.dram_tensor("cos_t", [128, S], F32, kind="ExternalInput")
    sin_t = nc.dram_tensor("sin_t", [128, S], F32, kind="ExternalInput")
    keysc = nc.dram_tensor("keysc", [128, 17], F32, kind="ExternalInput")
    diagm = nc.dram_tensor("diagm", [128, 256], F32, kind="ExternalInput")
    i128 = nc.dram_tensor("i128", [128, 128], F32, kind="ExternalInput")
    e4 = nc.dram_tensor("e4", [4, 128], F32, kind="ExternalInput")
    onesf = nc.dram_tensor("onesf", [1, 128], F32, kind="ExternalInput")
    cvec = nc.dram_tensor("cvec", [128, 8], F32, kind="ExternalInput")
    ksraw = nc.dram_tensor("ksraw", [128, KV * 640], F32, kind="ExternalInput")
    out = nc.dram_tensor("out", [NT, 128, D], F32, kind="ExternalOutput")

    kind_stash = "ExternalOutput" if debug else "Internal"
    qstash = nc.dram_tensor("qstash", [H, 128, S], BF16, kind=kind_stash)
    kstash = nc.dram_tensor("kstash", [KV, 128, S], BF16, kind=kind_stash)
    vstash = nc.dram_tensor("vstash", [NT, 128, KV, HD], BF16, kind=kind_stash)
    gstash = nc.dram_tensor("gstash", [NT, 128, D], BF16, kind=kind_stash)
    ostash = nc.dram_tensor("ostash", [NT, 128, H, 128], BF16, kind=kind_stash)
    xdbg = None
    if debug:
        xdbg = nc.dram_tensor("xdbg", [2, 128, ND, HALF], BF16, kind="ExternalOutput")

    env = {"nc": nc, "debug": debug, "x_feat": x_feat, "x_tok": x_tok, "ksraw": ksraw,
           "cos_t": cos_t, "sin_t": sin_t,
           "wq_blob": wq_blob, "wk_blob": wk_blob, "wv_rhs": wv_rhs, "wo_t": wo_t,
           "out": out, "qstash": qstash, "kstash": kstash, "vstash": vstash,
           "gstash": gstash, "ostash": ostash, "xdbg": xdbg}

    with ExitStack() as ctx:
        def sb(name, shape, dt):
            t = ctx.enter_context(nc.sbuf_tensor(name, shape, dt))
            env[name] = t
            return t

        sb("keysc_sb", [128, 17], F32)
        sb("diag_sb", [128, 256], BF16)
        sb("i128_sb", [128, 128], F32)
        sb("e4_sb", [4, 128], BF16)
        sb("onesf_sb", [1, 128], F32)
        sb("ones_sb", [128, 128], BF16)
        sb("cvec_sb", [128, 8], F32)
        sb("bpad_sb", [128, D], BF16)
        sb("ab_sb", [128, ND, 128], BF16)

        def sem(name):
            c = Ctr(ctx.enter_context(nc.semaphore(name)))
            env[name] = c
            return c

        for n in ["ld_c", "init", "ld_x", "ld_x1", "ld_x2", "ld_x3", "vsq", "tsq", "stev", "vst", "sst",
                  "tbc", "vnorm", "tmid", "smid", "ld_lg", "tlt", "vga", "sge",
                  "vgt", "tgT", "sgT", "tgb", "vpg", "td", "ssil", "st_g0", "st_g1", "st_g2", "st_g3", "ld_w0", "ld_w1", "tpr", "sev", "ld_sw", "vrope",
                  "st_p0", "st_p1", "ld_v0", "ld_v1", "ld_v2", "ld_sw0", "ld_sw1", "tv", "svev", "st_v",
                  "ld_kv", "ld_ks", "vks", "ld_q0", "ld_q1", "tS", "sexp", "vdg", "tPV", "vON",
                  "st_o0", "st_o1", "st_o2", "st_o3", "sON0", "sON1", "vON0", "vON1", "ld_wo0", "ld_wo1", "ld_at0", "ld_at1", "ld_xg0",
                  "ld_xg1", "tW", "vP4", "st_u0", "st_u1", "st_dbg",
                  "sbc", "ld_k2", "ld_vv", "ld_vv2", "vself", "sself"]:
            sem(n)

        VE = Eng(nc.vector, env["vself"])
        SE = Eng(nc.scalar, env["sself"])
        env["VE"] = VE
        env["SE"] = SE

        ld_c, init = env["ld_c"], env["init"]

        env["const_pairs"] = [("keysc_sb", keysc), ("diag_sb", diagm),
                              ("i128_sb", i128), ("e4_sb", e4),
                              ("onesf_sb", onesf), ("cvec_sb", cvec),
                              ("bpad_sb", b_pad), ("ab_sb", a_blob)]
        with nc.Block() as blk:
            @blk.vector
            def _(v):
                VE.run("memset", env["ones_sb"][:, :], 1.0, m=init)

        e1 = ExitStack()
        with e1:
            def sb1(name, shape, dt):
                t = e1.enter_context(nc.sbuf_tensor(name, shape, dt))
                env[name] = t
                return t

            sb1("cos_sb", [128, S], F32)
            sb1("sin_sb", [128, S], F32)
            sb1("xn_sb", [128, ND, HALF], BF16)
            sb1("sq_sb", [128, 2, HALF], BF16)
            sb1("st_sb", [1, 3 * HALF], F32)
            sb1("wb_sb", [128, 2, ND, 128], BF16)
            sb1("wv_sb", [128, 3, 4, KV * HD], BF16)
            sb1("qe_sb", [128, 2, HALF], BF16)
            sb1("qs_sb", [128, 2, HALF], BF16)
            sb1("rt1_sb", [128, HALF], F32)
            sb1("rt2_sb", [128, HALF], F32)
            sb1("qr_sb", [128, 2, HALF], BF16)
            sb1("ve_sb", [128, 4, KV * HD], BF16)
            sb1("mid_sb", [128, HALF], F32)
            sb1("lg4_sb", [4, HALF], F32)
            sb1("gt_sb", [128, 32], F32)
            sb1("gTb_sb", [4, HALF], BF16)
            sb1("negm1_sb", [128, 8], F32)
            sb1("negm2_sb", [128, 8], F32)
            sb1("gtmp_sb", [128, 32], F32)
            sb1("keep_sb", [128, 32], F32)
            sb1("et_sb", [128, 32], F32)
            sb1("gs_sb", [128, 8], F32)
            sb1("grs_sb", [128, 8], F32)
            sb1("pg_sb", [128, HALF], BF16)
            sb1("gsil_sb", [128, 4, 1024], BF16)

            env["kpre_blob"] = wk_blob
            for half in range(2):
                with nc.Block() as blk0:
                    env["blk"] = blk0
                    _p0(half, env)
                    _ffn_mid(half, env)
                env.pop("blk", None)
                _ffn_gates_pre(half, env)
                _proj_rope(half, KV, wk_blob, kstash, env, light=True,
                           single=True)
                _ffn_gates_post(half, env)
                _preload_blobs(half, env, wq_blob, 2)
                _vproj(half, env)
                with nc.psum_tensor(f"ps_dl_{half}", [128, 2, 1024], F32) as ps_dl:
                    hk = _ffn_delta_hooks(half, env, ps_dl)
                    _proj_rope(half, H, wq_blob, qstash, env, light=False,
                               hooks=hk)

        with nc.Block() as blk:
            @blk.scalar
            def _(s):
                for n in ["st_p0", "st_p1", "st_v", "st_g0", "st_g1", "st_g2", "st_g3"]:
                    s.wait_ge(env[n].h, env[n].v)
                if debug:
                    s.wait_ge(env["st_dbg"].h, env["st_dbg"].v)
        nc.all_engine_barrier()

        env["wo_sb"] = ctx.enter_context(nc.sbuf_tensor("wo_sb", [128, 32, 1024], BF16))
        e2 = ExitStack()
        with e2:
            for name, shape, dt in [
                ("k_sb", [128, KV, S], BF16),
                ("kscal_sb", [128, KV, 640], BF16),
                ("v_sb", [128, NT, KV, HD], BF16),
                ("q_sb", [128, 2, 2, S], BF16),
                ("p_sb", [128, 8, 2, 512], BF16),
                ("rd_sb", [128, 2, 512], F32),
                ("o_sb", [128, 4, 512], BF16),
            ]:
                env[name] = e2.enter_context(nc.sbuf_tensor(name, shape, dt))
            _p3(env)

        with nc.Block() as blk:
            @blk.scalar
            def _(s):
                for n in ["st_o0", "st_o1", "st_o2", "st_o3"]:
                    s.wait_ge(env[n].h, env[n].v)
        nc.all_engine_barrier()

        e3 = ExitStack()
        with e3:
            for name, shape, dt in [
                ("wo2_sb", [128, 32, 1024], BF16),
                ("at_sb", [128, 2, H, 128], BF16),
                ("x4_sb", [128, 2, 1024], F32),
                ("g4_sb", [128, 2, 1024], BF16),
                ("o4_sb", [128, 2, 1024], F32),
                ("r4_sb", [128, 1024], F32),
            ]:
                env[name] = e3.enter_context(nc.sbuf_tensor(name, shape, dt))
            _p4(env)

    return nc


def _p0(half, env):
    nc = env["nc"]
    debug = env["debug"]
    VE, SE = env["VE"], env["SE"]
    (ld_x, vsq, tsq, stev, vst, sst, tbc, vnorm, ld_c, init, tpr, sev,
     st_dbg, sbc) = (env[n] for n in
                ["ld_x", "vsq", "tsq", "stev", "vst", "sst", "tbc", "vnorm",
                 "ld_c", "init", "tpr", "sev", "st_dbg", "sbc"])
    xn_sb, sq_sb, st_sb = (
        env[n] for n in ["xn_sb", "sq_sb", "st_sb"])
    tmp_sb = sq_sb
    ones_sb, onesf_sb, cvec_sb = env["ones_sb"], env["onesf_sb"], env["cvec_sb"]
    x_feat = env["x_feat"]
    t0 = half * HALF
    b_tsq = tsq.v
    b_vsq = vsq.v
    b_lx = [env[n].v for n in ["ld_x", "ld_x1", "ld_x2", "ld_x3"]]
    b_tpr = tpr.v
    b_sev = sev.v

    blk = env["blk"]
    with (
        nc.psum_tensor(f"ps_stat_{half}", [1, 2, HALF], F32) as ps_stat,
        nc.psum_tensor(f"ps_bc_{half}", [128, 2, HALF], F32) as ps_bc,
    ):
        @blk.gpsimd
        def _(g):
            if half == 1:
                g.wait_ge(tpr.h, b_tpr)
                g.wait_ge(env["tv"].h, env["tv"].v)
                g.wait_ge(env["tmid"].h, env["tmid"].v)
                g.wait_ge(tsq.h, b_tsq)
                g.wait_ge(vsq.h, b_vsq)
                if debug:
                    g.wait_ge(st_dbg.h, st_dbg.v)
            for q in range(4):
                lx = env[["ld_x", "ld_x1", "ld_x2", "ld_x3"][q]]
                g.dma_start(
                    xn_sb[:, 8 * q:8 * (q + 1), :],
                    x_feat[:, 8 * q:8 * (q + 1), t0:t0 + HALF],
                ).then_inc(lx.h, 16)
                lx.v += 16
            if half == 0:
                g.dma_start(env["cos_sb"][:, :], env["cos_t"][:, :]).then_inc(ld_c.h, 16)
                g.dma_start(env["sin_sb"][:, :], env["sin_t"][:, :]).then_inc(ld_c.h, 16)
                ld_c.v += 32
                for dn, srct in env["const_pairs"]:
                    dst = env[dn]
                    idx = tuple(slice(None) for _ in dst.shape)
                    sidx = tuple(slice(None) for _ in srct.shape)
                    g.dma_start(dst[idx], srct[sidx]).then_inc(ld_c.h, 16)
                    ld_c.v += 16
            kblob = env["kpre_blob"]
            env["qblob_base"] = [env["ld_w0"].v, env["ld_w1"].v]
            for f in range(2):
                lw = env["ld_w0"] if f % 2 == 0 else env["ld_w1"]
                g.dma_start(env["wb_sb"][:, f % 2, :, :],
                            kblob[f]).then_inc(lw.h, 16)
                lw.v += 16
            env["qblob_pre"] = 2
            wv_r = env["wv_rhs"].rearrange("c p f -> p c f")
            env["vblob_base"] = [env[n].v for n in ["ld_v0", "ld_v1", "ld_v2"]]
            for gi in range(3):
                lv = env[["ld_v0", "ld_v1", "ld_v2"][gi]]
                g.dma_start(env["wv_sb"][:, gi, :, :],
                            wv_r[:, 4 * gi:4 * (gi + 1), :]).then_inc(lv.h, 16)
                lv.v += 16
            env["vpre"] = 3

        @blk.vector
        def _(v):
            for c in range(ND):
                if c % 8 == 0:
                    lx = env[["ld_x", "ld_x1", "ld_x2", "ld_x3"][c // 8]]
                    VE.wait(lx, b_lx[c // 8] + 16)
                if c >= 2:
                    VE.wait(tsq, b_tsq + c - 1)
                VE.run("tensor_mul", sq_sb[:, c % 2, :], xn_sb[:, c, :],
                       xn_sb[:, c, :], m=vsq)

        @blk.tensor
        def _(t):
            t.wait_ge(init.h, 1)
            t.wait_ge(ld_c.h, ld_c.v)
            if half == 1:
                t.wait_ge(sev.h, b_sev)
            for c in range(ND):
                if c % 8 == 0:
                    lx = env[["ld_x", "ld_x1", "ld_x2", "ld_x3"][c // 8]]
                    t.wait_ge(lx.h, b_lx[c // 8] + 16)
                for b in range(2):
                    t.matmul(ps_stat[:, 0, 512 * b:512 * (b + 1)],
                             ones_sb[:, 0:1],
                             xn_sb[:, c, 512 * b:512 * (b + 1)],
                             start=(c == 0), stop=(c == ND - 1))
                t.wait_ge(vsq.h, b_vsq + c + 1)
                for b in range(2):
                    ins = t.matmul(ps_stat[:, 1, 512 * b:512 * (b + 1)],
                                   ones_sb[:, 0:1],
                                   sq_sb[:, c % 2, 512 * b:512 * (b + 1)],
                                   start=(c == 0), stop=(c == ND - 1))
                tsq.inc(ins)

        @blk.scalar
        def _(s):
            SE.wait(tsq, b_tsq + ND)
            SE.run("copy", st_sb[:, 0:HALF], ps_stat[:, 0, :])
            SE.run("copy", st_sb[:, HALF:2 * HALF], ps_stat[:, 1, :], m=stev)

        @blk.vector
        def _(v):
            VE.wait(stev, stev.v)
            VE.run("tensor_scalar_mul", st_sb[:, 0:HALF], st_sb[:, 0:HALF], 1.0 / D)
            VE.run("tensor_mul", st_sb[:, 2 * HALF:3 * HALF],
                   st_sb[:, 0:HALF], st_sb[:, 0:HALF])
            VE.run("tensor_scalar", st_sb[:, HALF:2 * HALF], st_sb[:, HALF:2 * HALF],
                   1.0 / D, EPS, OP.mult, OP.add)
            VE.run("tensor_sub", st_sb[:, 2 * HALF:3 * HALF],
                   st_sb[:, HALF:2 * HALF], st_sb[:, 2 * HALF:3 * HALF], m=vst)

        @blk.scalar
        def _(s):
            SE.wait(vst, vst.v)
            SE.run("activation", st_sb[:, 2 * HALF:3 * HALF],
                   st_sb[:, 2 * HALF:3 * HALF], AF.Ln)
            SE.run("activation", st_sb[:, HALF:2 * HALF],
                   st_sb[:, 2 * HALF:3 * HALF], AF.Exp,
                   scale=cvec_sb[0:1, CV_NEGHALF:CV_NEGHALF + 1], m=sst)

        @blk.tensor
        def _(t):
            t.wait_ge(sst.h, sst.v)
            for b in range(2):
                t.matmul(ps_bc[:, 0, 512 * b:512 * (b + 1)], onesf_sb[:, :],
                         st_sb[:, 512 * b:512 * (b + 1)],
                         start=True, stop=True)
                ins = t.matmul(ps_bc[:, 1, 512 * b:512 * (b + 1)], onesf_sb[:, :],
                               st_sb[:, HALF + 512 * b:HALF + 512 * (b + 1)],
                               start=True, stop=True)
            tbc.inc(ins)

        @blk.scalar
        def _(s):
            SE.wait(tbc, tbc.v)
            SE.run("copy", sq_sb[:, 0, :], ps_bc[:, 0, :])
            SE.run("copy", sq_sb[:, 1, :], ps_bc[:, 1, :], m=sbc)

        @blk.vector
        def _(v):
            VE.wait(sbc, sbc.v)
            for c in range(ND):
                VE.run("tensor_sub", xn_sb[:, c, :], xn_sb[:, c, :], sq_sb[:, 0, :])
                VE.run("tensor_mul", xn_sb[:, c, :], xn_sb[:, c, :], sq_sb[:, 1, :],
                       m=vnorm)

        if debug:
            @blk.scalar
            def _(s):
                SE.wait(vnorm, vnorm.v)
                SE.run("dma_start", env["xdbg"][half], xn_sb[:, :, :],
                       m=st_dbg, dma=True)


def _ffn_mid(half, env):
    nc = env["nc"]
    SE = env["SE"]
    vnorm, tmid, smid, ld_lg = (env[n] for n in ["vnorm", "tmid", "smid", "ld_lg"])
    xn_sb, ab_sb, mid_sb, lg4_sb = (env[n] for n in
                                    ["xn_sb", "ab_sb", "mid_sb", "lg4_sb"])

    blk = env["blk"]
    with (
        nc.psum_tensor(f"ps_mid_{half}", [128, HALF], F32) as ps_mid,
    ):
        @blk.tensor
        def _(t):
            b_vn = vnorm.v - ND
            for b in range(2):
                for c in range(ND):
                    if b == 0:
                        t.wait_ge(vnorm.h, b_vn + c + 1)
                    ins = t.matmul(ps_mid[:, 512 * b:512 * (b + 1)], ab_sb[:, c, :],
                                   xn_sb[:, c, 512 * b:512 * (b + 1)],
                                   start=(c == 0), stop=(c == ND - 1))
            tmid.inc(ins)

        @blk.scalar
        def _(s):
            SE.wait(tmid, tmid.v)
            SE.wait(env["ld_lg"], env["ld_lg"].v)
            SE.run("copy", mid_sb[:, :], ps_mid[:, :], m=smid)

        @blk.sync
        def _(sy):
            sy.wait_ge(smid.h, smid.v)
            sy.dma_start(lg4_sb[:, :], mid_sb[16:20, :]).then_inc(ld_lg.h, 16)
            ld_lg.v += 16


def _ffn_gates_pre(half, env):
    nc = env["nc"]
    VE, SE = env["VE"], env["SE"]
    (ld_lg, tlt, vga, sge, vgt) = (
        env[n] for n in ["ld_lg", "tlt", "vga", "sge", "vgt"])
    (lg4_sb, i128_sb, gt_sb, negm1_sb, negm2_sb, gtmp_sb, keep_sb, et_sb,
     gs_sb, grs_sb) = (
        env[n] for n in ["lg4_sb", "i128_sb", "gt_sb", "negm1_sb", "negm2_sb",
                         "gtmp_sb", "keep_sb", "et_sb", "gs_sb", "grs_sb"])

    with (
        nc.psum_tensor(f"ps_lt_{half}", [128, 32], F32) as ps_lt,
        nc.Block() as blk,
    ):
        @blk.tensor
        def _(t):
            t.wait_ge(ld_lg.h, ld_lg.v)
            for tt in range(8):
                ins = t.matmul(ps_lt[:, 4 * tt:4 * (tt + 1)],
                               lg4_sb[:, 128 * tt:128 * (tt + 1)],
                               i128_sb[0:4, 0:4], start=True, stop=True)
            tlt.inc(ins)

        @blk.vector
        def _(v):
            VE.wait(tlt, tlt.v)
            for tt in range(8):
                sl = ps_lt[:, 4 * tt:4 * (tt + 1)]
                VE.run("reduce_max", negm1_sb[:, tt:tt + 1], sl, AX.X, negate=True)
                VE.run("tensor_scalar", gtmp_sb[:, 4 * tt:4 * (tt + 1)], sl,
                       negm1_sb[:, tt:tt + 1], 0.0, OP.add, OP.is_ge)
                VE.run("scalar_tensor_tensor", gtmp_sb[:, 4 * tt:4 * (tt + 1)],
                       gtmp_sb[:, 4 * tt:4 * (tt + 1)], -1e30, sl,
                       OP.mult, OP.add)
                VE.run("reduce_max", negm2_sb[:, tt:tt + 1],
                       gtmp_sb[:, 4 * tt:4 * (tt + 1)], AX.X, negate=True)
                VE.run("tensor_scalar", keep_sb[:, 4 * tt:4 * (tt + 1)], sl,
                       negm2_sb[:, tt:tt + 1], 0.0, OP.add, OP.is_ge,
                       m=(vga if tt == 7 else None))

        @blk.scalar
        def _(s):
            SE.wait(vga, vga.v)
            for tt in range(8):
                SE.run("activation", et_sb[:, 4 * tt:4 * (tt + 1)],
                       ps_lt[:, 4 * tt:4 * (tt + 1)], AF.Exp,
                       bias=negm1_sb[:, tt:tt + 1],
                       m=(sge if tt == 7 else None))

        @blk.vector
        def _(v):
            VE.wait(sge, sge.v)
            VE.run("tensor_mul", et_sb[:, :], et_sb[:, :], keep_sb[:, :])
            for tt in range(8):
                VE.run("reduce_sum", gs_sb[:, tt:tt + 1],
                       et_sb[:, 4 * tt:4 * (tt + 1)], AX.X)
            VE.run("reciprocal", grs_sb[:, :], gs_sb[:, :])
            for tt in range(8):
                VE.run("tensor_scalar_mul", gt_sb[:, 4 * tt:4 * (tt + 1)],
                       et_sb[:, 4 * tt:4 * (tt + 1)], grs_sb[:, tt:tt + 1],
                       m=(vgt if tt == 7 else None))


def _ffn_gates_post(half, env):
    nc = env["nc"]
    VE, SE = env["VE"], env["SE"]
    (vnorm, vgt, tgT, sgT, tgb, vpg, sev) = (
        env[n] for n in ["vnorm", "vgt", "tgT", "sgT", "tgb", "vpg", "sev"])
    gt_sb, gTb_sb, i128_sb, e4_sb, pg_sb, mid_sb = (
        env[n] for n in ["gt_sb", "gTb_sb", "i128_sb", "e4_sb", "pg_sb",
                         "mid_sb"])

    with (
        nc.psum_tensor(f"ps_gT_{half}", [4, HALF], F32) as ps_gT,
        nc.Block() as blk,
    ):
        @blk.tensor
        def _(t):
            t.wait_ge(vgt.h, vgt.v)
            t.wait_ge(sev.h, sev.v)
            for tt in range(8):
                ins = t.matmul(ps_gT[:, 128 * tt:128 * (tt + 1)],
                               gt_sb[:, 4 * tt:4 * (tt + 1)], i128_sb[:, :],
                               start=True, stop=True)
            tgT.inc(ins)

        @blk.scalar
        def _(s):
            SE.wait(tgT, tgT.v)
            SE.run("copy", gTb_sb[:, :], ps_gT[:, :], m=sgT)

    with (
        nc.psum_tensor(f"ps_gb_{half}", [128, HALF], F32) as ps_gb,
        nc.Block() as blk,
    ):
        @blk.tensor
        def _(t):
            t.wait_ge(sgT.h, sgT.v)
            t.wait_ge(vnorm.h, vnorm.v)
            for b in range(2):
                ins = t.matmul(ps_gb[:, 512 * b:512 * (b + 1)], e4_sb[:, :],
                               gTb_sb[:, 512 * b:512 * (b + 1)], start=True, stop=True)
            tgb.inc(ins)

        @blk.vector
        def _(v):
            VE.wait(tgb, tgb.v)
            VE.run("tensor_mul", pg_sb[:, :], mid_sb[:, :], ps_gb[:, :], m=vpg)


def _ffn_delta_hooks(half, env, ps_dl):
    SE = env["SE"]
    (vpg, td, ssil, svev, sev) = (env[n] for n in
                                  ["vpg", "td", "ssil", "svev", "sev"])
    st_g = [env[f"st_g{i}"] for i in range(4)]
    pg_sb, gsil_sb, bpad_sb, cvec_sb = (
        env[n] for n in ["pg_sb", "gsil_sb", "bpad_sb", "cvec_sb"])
    gstash = env["gstash"]
    b_td = td.v
    b_sil = ssil.v
    b_g = [c.v for c in st_g]

    def t_hook(t, f):
        if f >= 32:
            return
        if f == 0:
            t.wait_ge(vpg.h, vpg.v)
            t.wait_ge(svev.h, svev.v)
            t.wait_ge(sev.h, sev.v)
        for k in (f,):
            tt, o2 = k // 4, k % 4
            if k >= 2:
                t.wait_ge(ssil.h, b_sil + k - 1)
            for b in range(2):
                ins = t.matmul(ps_dl[:, k % 2, 512 * b:512 * (b + 1)],
                               pg_sb[:, 128 * tt:128 * (tt + 1)],
                               bpad_sb[:, 1024 * o2 + 512 * b:
                                       1024 * o2 + 512 * (b + 1)],
                               start=True, stop=True)
            td.inc(ins)

    def s_hook(s, f):
        if f >= 32:
            return
        for k in (f,):
            tt, o2 = k // 4, k % 4
            stg = st_g[k % 4]
            SE.wait(td, b_td + k + 1)
            SE.wait(stg, b_g[k % 4] + 16 * (k // 4))
            SE.run("activation", gsil_sb[:, k % 4, :], ps_dl[:, k % 2, :],
                   AF.Silu, scale=cvec_sb[:, CV_FOUR:CV_FOUR + 1], m=ssil)
            SE.run("dma_start",
                   gstash[half * 8 + tt][:, 1024 * o2:1024 * (o2 + 1)],
                   gsil_sb[:, k % 4, :], m=stg, dma=True)

    return {"tensor": t_hook, "scalar": s_hook}


def _proj_rope(half, NF, blob, stash, env, light=False, single=False, hooks=None):
    nc = env["nc"]
    VE, SE = env["VE"], env["SE"]
    (ld_w0, ld_w1, tpr, sev, ld_sw0, ld_sw1, vrope, st_p0, st_p1, vnorm,
     ld_c) = (
        env[n] for n in ["ld_w0", "ld_w1", "tpr", "sev", "ld_sw0", "ld_sw1",
                         "vrope", "st_p0", "st_p1", "vnorm", "ld_c"])
    (wb_sb, qe_sb, qs_sb, rt1_sb, rt2_sb, qr_sb, xn_sb, cos_sb, sin_sb) = (
        env[n] for n in ["wb_sb", "qe_sb", "qs_sb", "rt1_sb", "rt2_sb",
                         "qr_sb", "xn_sb", "cos_sb", "sin_sb"])
    t0 = half * HALF
    pre = env.pop("qblob_pre", 0)
    if pre:
        b_w = env.pop("qblob_base")
    else:
        b_w = [ld_w0.v, ld_w1.v]
    b_tpr = tpr.v
    b_sev = sev.v
    b_sw = [ld_sw0.v, ld_sw1.v]
    b_rp = vrope.v
    b_p = [st_p0.v, st_p1.v]

    es_p = ExitStack()
    with es_p, nc.Block() as blk:
        ps_pa = es_p.enter_context(
            nc.psum_tensor(f"ps_pa_{half}_{NF}", [128, HALF], F32))
        if single:
            ps_p = [ps_pa, ps_pa]
        else:
            ps_pb = es_p.enter_context(
                nc.psum_tensor(f"ps_pb_{half}_{NF}", [128, HALF], F32))
            ps_p = [ps_pa, ps_pb]
        @blk.gpsimd
        def _(g):
            def load_blob(f):
                par = f % 2
                lw = ld_w0 if par == 0 else ld_w1
                g.wait_ge(tpr.h, b_tpr + max(0, f - 1))
                g.dma_start(wb_sb[:, par, :, :], blob[f]).then_inc(lw.h, 16)
                lw.v += 16

            for f in range(pre, NF):
                load_blob(f)

        @blk.sync
        def _(sy):
            for f in range(NF):
                par = f % 2
                lsw = ld_sw0 if par == 0 else ld_sw1
                sy.wait_ge(sev.h, b_sev + f + 1)
                sy.wait_ge(vrope.h, b_rp + max(0, f - 1))
                sy.dma_start(qs_sb[0:64, par, :],
                             qe_sb[64:128, par, :]).then_inc(lsw.h, 16)
                sy.dma_start(qs_sb[64:128, par, :],
                             qe_sb[0:64, par, :]).then_inc(lsw.h, 16)
                lsw.v += 32

        @blk.tensor
        def _(t):
            t.wait_ge(vnorm.h, vnorm.v)
            if light:
                t.wait_ge(env["smid"].h, env["smid"].v)
            else:
                t.wait_ge(env["vpg"].h, env["vpg"].v)
            t.wait_ge(env["svev"].h, env["svev"].v)
            t.wait_ge(sev.h, b_sev)
            for f in range(NF):
                par = f % 2
                lw = ld_w0 if par == 0 else ld_w1
                t.wait_ge(lw.h, b_w[par] + 16 * (f // 2 + 1))
                if single:
                    if f >= 1:
                        t.wait_ge(sev.h, b_sev + f)
                elif f >= 2:
                    t.wait_ge(sev.h, b_sev + f - 1)
                for c in range(ND):
                    for b in range(2):
                        ins = t.matmul(ps_p[0 if single else par][:, 512 * b:512 * (b + 1)],
                                       wb_sb[:, par, c, :],
                                       xn_sb[:, c, 512 * b:512 * (b + 1)],
                                       start=(c == 0), stop=(c == ND - 1))
                tpr.inc(ins)
                if hooks is not None:
                    hooks["tensor"](t, f)

        @blk.scalar
        def _(s):
            for f in range(NF + 1):
                if f < NF:
                    par = f % 2
                    SE.wait(tpr, b_tpr + f + 1)
                    lsw = ld_sw0 if par == 0 else ld_sw1
                    SE.wait(lsw, b_sw[par] + 32 * (f // 2))
                    SE.wait(vrope, b_rp + max(0, f - 1))
                    SE.run("copy", qe_sb[:, par, :], ps_p[0 if single else par][:, :], m=sev)
                if f >= 1:
                    fp = f - 1
                    par = fp % 2
                    stp = st_p0 if par == 0 else st_p1
                    SE.wait(vrope, b_rp + fp + 1)
                    SE.run("dma_start", stash[fp][:, t0:t0 + HALF],
                           qr_sb[:, par, :], m=stp, dma=True)
                if hooks is not None and f < NF:
                    hooks["scalar"](s, f)

        @blk.vector
        def _(v):
            VE.wait(ld_c, ld_c.v)
            for f in range(NF):
                par = f % 2
                stp = st_p0 if par == 0 else st_p1
                lsw = ld_sw0 if par == 0 else ld_sw1
                VE.wait(lsw, b_sw[par] + 32 * (f // 2 + 1))
                VE.wait(stp, b_p[par] + 16 * (f // 2))
                VE.run("tensor_mul", rt1_sb[:, :], qe_sb[:, par, :],
                       cos_sb[:, t0:t0 + HALF])
                VE.run("tensor_mul", rt2_sb[:, :], qs_sb[:, par, :],
                       sin_sb[:, t0:t0 + HALF])
                VE.run("tensor_add", qr_sb[:, par, :], rt1_sb[:, :], rt2_sb[:, :],
                       m=vrope)


def _preload_blobs(half, env, blob, n):
    nc = env["nc"]
    ld_w0, ld_w1, tpr = env["ld_w0"], env["ld_w1"], env["tpr"]
    env["qblob_base"] = [ld_w0.v, ld_w1.v]
    wb_sb = env["wb_sb"]
    with nc.Block() as blk:
        @blk.gpsimd
        def _(g):
            for f in range(n):
                par = f % 2
                lw = ld_w0 if par == 0 else ld_w1
                g.wait_ge(tpr.h, tpr.v)
                g.dma_start(wb_sb[:, par, :, :], blob[f]).then_inc(lw.h, 16)
                lw.v += 16
    env["qblob_pre"] = n


def _vproj(half, env):
    nc = env["nc"]
    SE = env["SE"]
    ld_v0, ld_v1, ld_v2, tv, svev, st_v, vnorm = (
        env[n] for n in ["ld_v0", "ld_v1", "ld_v2", "tv", "svev", "st_v",
                         "vnorm"])
    lvs = [ld_v0, ld_v1, ld_v2]
    wv_sb, ve_sb, xn_sb = env["wv_sb"], env["ve_sb"], env["xn_sb"]
    wv_rhs, vstash = env["wv_rhs"], env["vstash"]
    pre_v = env.pop("vpre", 0)
    if pre_v:
        b_v = env.pop("vblob_base")
    else:
        b_v = [ld_v0.v, ld_v1.v, ld_v2.v]
    b_tv = tv.v
    b_sv = svev.v
    b_stv = st_v.v

    with (
        nc.psum_tensor(f"ps_v_{half}", [128, 4, KV * HD], F32) as ps_v,
        nc.Block() as blk,
    ):
        @blk.gpsimd
        def _(g):
            for gi in range(pre_v, 2 * ND // 4):
                par = gi % 3
                lv = lvs[par]
                cc = (gi * 4) % ND
                g.wait_ge(tv.h, b_tv + max(0, 4 * (gi - 2)))
                g.dma_start(wv_sb[:, par, :, :],
                            wv_rhs.rearrange("c p f -> p c f")[:, cc:cc + 4, :]
                            ).then_inc(lv.h, 16)
                lv.v += 16

        @blk.tensor
        def _(t):
            t.wait_ge(vnorm.h, vnorm.v)
            t.wait_ge(env["vpg"].h, env["vpg"].v)
            t.wait_ge(env["sev"].h, env["sev"].v)
            t.wait_ge(svev.h, b_sv)
            for grp in range(2):
                if grp == 1:
                    t.wait_ge(svev.h, b_sv + 4)
                for c in range(ND):
                    k = grp * ND + c
                    gi = k // 4
                    par = gi % 3
                    lv = lvs[par]
                    t.wait_ge(lv.h, b_v[par] + 16 * (gi // 3 + 1))
                    for tl in range(4):
                        tt = grp * 4 + tl
                        for b in range(2):
                            ins = t.matmul(ps_v[:, tl, 512 * b:512 * (b + 1)],
                                           xn_sb[:, c, 128 * tt:128 * (tt + 1)],
                                           wv_sb[:, par, c % 4, 512 * b:512 * (b + 1)],
                                           start=(c == 0), stop=(c == ND - 1))
                    tv.inc(ins)

        @blk.scalar
        def _(s):
            for grp in range(2):
                SE.wait(tv, b_tv + ND * (grp + 1))
                SE.wait(st_v, b_stv + (64 if grp == 1 else 0))
                for tl in range(4):
                    tt = grp * 4 + tl
                    SE.run("copy", ve_sb[:, tl, :], ps_v[:, tl, :], m=svev)
                    SE.run("dma_start", vstash[half * 8 + tt], ve_sb[:, tl, :],
                           m=st_v, dma=True)


def _p3(env):
    nc = env["nc"]
    VE, SE = env["VE"], env["SE"]
    (ld_kv, ld_q0, ld_q1, tS, sexp, vdg, tPV, ld_wo0, ld_c, init) = (
        env[n] for n in ["ld_kv", "ld_q0", "ld_q1", "tS", "sexp", "vdg", "tPV",
                         "ld_wo0", "ld_c", "init"])
    sON = [env["sON0"], env["sON1"]]
    vON = [env["vON0"], env["vON1"]]
    st_o = [env["st_o0"], env["st_o1"], env["st_o2"], env["st_o3"]]
    k_sb, kscal_sb, v_sb, q_sb, p_sb, rd_sb, o_sb = (
        env[n] for n in ["k_sb", "kscal_sb", "v_sb", "q_sb", "p_sb", "rd_sb",
                         "o_sb"])
    keysc_sb, diag_sb, ones_sb, wo_sb, cvec_sb = (
        env[n] for n in ["keysc_sb", "diag_sb", "ones_sb", "wo_sb", "cvec_sb"])
    kstash, vstash, qstash, ostash, wo_t = (
        env[n] for n in ["kstash", "vstash", "qstash", "ostash", "wo_t"])
    ksraw_r = env["ksraw"]

    flat = []
    pb = 0
    for (pr, qt, bands) in PAIR_SCHED:
        lst = []
        for (j, lo, hi) in bands:
            lst.append((pb, j, lo, hi))
            pb += 1
        flat.append((pr, qt, lst))
    n_band = pb
    dplan = {}
    dc = 0
    for (pr, qt, lst) in flat:
        for (g, j, lo, hi) in lst:
            if j >= 4 * qt:
                dc += 1
            dplan[g] = dc

    pv_order = []
    for (pr, qt, lst) in flat:
        seg = [(e, pr, qt) for e in lst]
        pv_order += [seg[1], seg[2], seg[0]] + seg[3:]
    env["pv_order"] = pv_order
    pvdone = {}
    for pos, (e, pr, qt) in enumerate(pv_order):
        pvdone[e[0]] = 2 * (pos + 1)
    gslot = {}
    for si, (pr, qt, lst) in enumerate(flat):
        for (g, j, lo, hi) in lst:
            gslot[g] = (6 + si % 2) if j == 0 else g % 6
    env["gslot"] = gslot
    slot_wait = {}
    last_user = {}
    for (pr, qt, lst) in flat:
        for (g, j, lo, hi) in lst:
            slot = gslot[g]
            if slot in last_user:
                slot_wait[g] = pvdone[last_user[slot]]
            last_user[slot] = g
    env["slot_wait"] = slot_wait

    with (
        nc.psum_tensor("ps_S", [128, 2, 2, 512], F32) as ps_S,
        nc.psum_tensor("ps_o", [128, 2, 512], F32) as ps_o,
        nc.psum_tensor("ps_d3", [128, 2, 512], F32) as ps_d3,
        nc.Block() as blk,
    ):
        @blk.gpsimd
        def _(g):
            g.dma_start(kscal_sb[:, :, :].rearrange("p k t -> p (k t)"),
                        ksraw_r[:, :]).then_inc(env["ld_ks"].h, 16)
            env["ld_ks"].v += 16
            for hc in range(32):
                g.dma_start(wo_sb[:, hc, :],
                            wo_t[hc][:, 0:1024]).then_inc(ld_wo0.h, 16)
                ld_wo0.v += 16

        @blk.sync
        def _(sy):
            sy.dma_start(k_sb[:, 0:4, :],
                         kstash.rearrange("f p t -> p f t")[:, 0:4, :]
                         ).then_inc(ld_kv.h, 16)
            ld_kv.v += 16
            for pr in range(H // 2):
                lq = ld_q0 if pr % 2 == 0 else ld_q1
                if pr >= 2:
                    sy.wait_ge(tS.h, BANDS_PER_PAIR * (pr - 1))
                for X in range(2):
                    sy.dma_start(q_sb[:, pr % 2, X, :],
                                 qstash[2 * pr + X]).then_inc(lq.h, 16)
                    lq.v += 16
                if pr == 1:
                    sy.dma_start(k_sb[:, 4:8, :],
                                 kstash.rearrange("f p t -> p f t")[:, 4:8, :]
                                 ).then_inc(env["ld_k2"].h, 16)
                    env["ld_k2"].v += 16
                if pr >= 1:
                    for qt in range(4):
                        _ostash_write(sy, env, pr - 1, qt, vON, st_o, o_sb, ostash)
            for qt in range(4):
                _ostash_write(sy, env, H // 2 - 1, qt, vON, st_o, o_sb, ostash)

        @blk.tensor
        def _(t):
            t.wait_ge(init.h, 1)

            def emit_S(g, j, lo, hi, pr, qt):
                kvh = pr // 2
                L = hi - lo
                sp = g % 2
                if g >= 2:
                    t.wait_ge(sexp.h, g - 1)
                for X in range(2):
                    if qt == 1 and j <= 4:
                        bnd = BOUND - lo
                        t.matmul(ps_S[:, sp, X, 0:bnd],
                                 k_sb[:, kvh, 128 * j:128 * (j + 1)],
                                 q_sb[:, pr % 2, X, lo:lo + bnd],
                                 start=True, stop=False)
                        ins = t.matmul(ps_S[:, sp, X, bnd:L],
                                       kscal_sb[:, kvh, 128 * j:128 * (j + 1)],
                                       q_sb[:, pr % 2, X, lo + bnd:hi],
                                       start=False, stop=True)
                    else:
                        kk = kscal_sb if (qt >= 2 and j <= 4) else k_sb
                        ins = t.matmul(ps_S[:, sp, X, 0:L],
                                       kk[:, kvh, 128 * j:128 * (j + 1)],
                                       q_sb[:, pr % 2, X, lo:lo + L],
                                       start=True, stop=True)
                tS.inc(ins)

            def emit_PV(g, j, lo, hi, pr, qt):
                kvh = pr // 2
                L = hi - lo
                pp = env["gslot"][g]
                jlast = 4 * qt + 3
                if j >= 4 * qt:
                    t.wait_ge(vdg.h, dplan[g])
                else:
                    t.wait_ge(sexp.h, g + 1)
                segidx = 4 * pr + qt
                q0 = lo - 512 * qt
                first = (j == 1)
                if j == 0:
                    parts = [(0, 128), (128, 512)]
                else:
                    parts = [(q0, 512)]
                for X in range(2):
                    if first and segidx >= 1:
                        t.wait_ge(vON[X].h, segidx)
                    for (z0, z1) in parts:
                        t.matmul(ps_o[:, X, z0:z1], v_sb[:, j, kvh, :],
                                 p_sb[:, pp, X, z0 - q0:z1 - q0],
                                 start=first, stop=(j == jlast))
                        ins = t.matmul(ps_d3[:, X, z0:z1], ones_sb[:, :],
                                       p_sb[:, pp, X, z0 - q0:z1 - q0],
                                       start=first, stop=(j == jlast))
                    tPV.inc(ins)

            LAG = 4
            allb = []
            pv_order = env["pv_order"]
            for (pr, qt, lst) in flat:
                for e in lst:
                    allb.append((e, pr, qt))
            v_waited = [False, False]
            def pv_gate(e2):
                half_idx = 0 if e2[1] < 8 else 1
                if not v_waited[half_idx]:
                    lv = env["ld_vv"] if half_idx == 0 else env["ld_vv2"]
                    t.wait_ge(lv.h, 16)
                    v_waited[half_idx] = True
            for i, (e, pr, qt) in enumerate(allb):
                if qt == 0 and e[1] == 0:
                    lq = ld_q0 if pr % 2 == 0 else ld_q1
                    t.wait_ge(lq.h, 32 * (pr // 2 + 1))
                    t.wait_ge(env["vks"].h, pr // 2 + 1)
                emit_S(*e, pr, qt)
                if i >= LAG:
                    (e2, pr2, qt2) = pv_order[i - LAG]
                    pv_gate(e2)
                    emit_PV(*e2, pr2, qt2)
            for i in range(len(allb) - LAG, len(allb)):
                (e2, pr2, qt2) = pv_order[i]
                pv_gate(e2)
                emit_PV(*e2, pr2, qt2)

        @blk.scalar
        def _(s):
            vst_r = vstash.rearrange("t p k h -> p t k h")
            SE.run("dma_start", v_sb[:, 0:8, :, :], vst_r[:, 0:8, :, :],
                   m=env["ld_vv"], dma=True)
            SE.run("dma_start", v_sb[:, 8:16, :, :], vst_r[:, 8:16, :, :],
                   m=env["ld_vv2"], dma=True)
            SE.wait(ld_c, ld_c.v)
            pend = None

            def emit_rden(last_pb):
                SE.wait(tPV, 2 * (last_pb + 1))
                SE.run("activation", rd_sb[:, :, :], ps_d3[:, :, :], AF.Ln)
                SE.run("activation", rd_sb[:, :, :], rd_sb[:, :, :], AF.Exp,
                       scale=cvec_sb[:, CV_NEG1:CV_NEG1 + 1], m=env["sON0"])

            for (pr, qt, lst) in flat:
                for bi, (g, j, lo, hi) in enumerate(lst):
                    L = hi - lo
                    sp = g % 2
                    SE.wait(tS, g + 1)
                    if g in env["slot_wait"]:
                        SE.wait(tPV, env["slot_wait"][g])
                    SE.run("activation",
                           p_sb[:, env["gslot"][g], :, 0:L],
                           ps_S[:, sp, :, 0:L],
                           AF.Exp, scale=keysc_sb[:, 16:17], m=sexp)
                    if bi == 1 and pend is not None:
                        emit_rden(pend)
                        pend = None
                pend = lst[-1][0]
            emit_rden(pend)
        @blk.vector
        def _(v):
            VE.wait(ld_c, ld_c.v)
            VE.wait(ld_kv, 16)
            VE.wait(env["ld_ks"], 16)
            for kv in range(KV):
                if kv == 4:
                    VE.wait(env["ld_k2"], 16)
                VE.run("tensor_mul", kscal_sb[:, kv, :], k_sb[:, kv, 0:640],
                       kscal_sb[:, kv, :], m=env["vks"])
            for (pr, qt, lst) in flat:
                segidx = 4 * pr + qt
                for (g, j, lo, hi) in lst:
                    if j >= 4 * qt:
                        VE.wait(sexp, g + 1)
                        pp = env["gslot"][g]
                        VE.run("tensor_mul", p_sb[:, pp, :, 0:128],
                               p_sb[:, pp, :, 0:128],
                               diag_sb[:, :].rearrange("p (a x) -> p a x", a=2),
                               m=vdg)
                for X in range(2):
                    sgl = 2 * segidx + X
                    VE.wait(env["sON0"], segidx + 1)
                    VE.wait(st_o[sgl % 4], 16 * (sgl // 4))
                    VE.run("tensor_mul", o_sb[:, sgl % 4, :], ps_o[:, X, :],
                           rd_sb[:, X, :], m=vON[X])


def _ostash_write(sy, env, pr, qt, vON, st_o, o_sb, ostash):
    segidx = 4 * pr + qt
    for X in range(2):
        sgl = 2 * segidx + X
        sy.wait_ge(vON[X].h, segidx + 1)
        sto = st_o[sgl % 4]
        sy.dma_start(
            ostash.rearrange("t p h x -> p t h x")[:, 4 * qt:4 * qt + 4,
                                                   2 * pr + X, :],
            o_sb[:, sgl % 4, :].rearrange("p (t x) -> p t x", t=4),
        ).then_inc(sto.h, 16)
        sto.v += 16


def _p4(env):
    nc = env["nc"]
    VE, SE = env["VE"], env["SE"]
    (ld_wo0, ld_wo1, ld_at0, ld_at1, ld_xg0, ld_xg1, tW, vP4, st_u0,
     st_u1) = (env[n] for n in
               ["ld_wo0", "ld_wo1", "ld_at0", "ld_at1", "ld_xg0",
                "ld_xg1", "tW", "vP4", "st_u0", "st_u1"])
    wo_sb, wo2_sb, at_sb, x4_sb, g4_sb, o4_sb, r4_sb = (
        env[n] for n in ["wo_sb", "wo2_sb", "at_sb", "x4_sb", "g4_sb",
                         "o4_sb", "r4_sb"])
    wo_t, ostash, x_tok, gstash, out = (
        env[n] for n in ["wo_t", "ostash", "x_tok", "gstash", "out"])

    with (
        nc.psum_tensor("ps_w", [128, 2, 1024], F32) as ps_w,
        nc.Block() as blk,
    ):
        @blk.gpsimd
        def _(g):
            for q in range(1, 4):
                buf = wo_sb if q % 2 == 0 else wo2_sb
                lwo = ld_wo0 if q % 2 == 0 else ld_wo1
                g.wait_ge(tW.h, 16 * (q - 1))
                for hc in range(32):
                    g.dma_start(buf[:, hc, :],
                                wo_t[hc][:, 1024 * q:1024 * (q + 1)]
                                ).then_inc(lwo.h, 16)
                    lwo.v += 16

        @blk.sync
        def _(sy):
            for i in range(4):
                sy.wait_ge(env[f"st_g{i}"].h, env[f"st_g{i}"].v)
            k = 0
            for q in range(4):
                for tt in range(NT):
                    par = k % 2
                    lat = ld_at0 if par == 0 else ld_at1
                    lxg = ld_xg0 if par == 0 else ld_xg1
                    if k >= 2:
                        sy.wait_ge(tW.h, k - 1)
                        sy.wait_ge(vP4.h, k - 1)
                    sy.dma_start(at_sb[:, par, :, :], ostash[tt]).then_inc(lat.h, 16)
                    sy.dma_start(x4_sb[:, par, :],
                                 x_tok[tt][:, 1024 * q:1024 * (q + 1)]
                                 ).then_inc(lxg.h, 16)
                    sy.dma_start(g4_sb[:, par, :],
                                 gstash[tt][:, 1024 * q:1024 * (q + 1)]
                                 ).then_inc(lxg.h, 16)
                    lat.v += 16
                    lxg.v += 32
                    k += 1

        @blk.tensor
        def _(t):
            k = 0
            for q in range(4):
                lwo = ld_wo0 if q % 2 == 0 else ld_wo1
                t.wait_ge(lwo.h, 512 * (q // 2 + 1))
                buf = wo_sb if q % 2 == 0 else wo2_sb
                for tt in range(NT):
                    par = k % 2
                    lat = ld_at0 if par == 0 else ld_at1
                    t.wait_ge(lat.h, 16 * (k // 2 + 1))
                    if k >= 2:
                        t.wait_ge(vP4.h, k - 1)
                    for hc in range(32):
                        for b in range(2):
                            ins = t.matmul(ps_w[:, par, 512 * b:512 * (b + 1)],
                                           at_sb[:, par, hc, :],
                                           buf[:, hc, 512 * b:512 * (b + 1)],
                                           start=(hc == 0), stop=(hc == 31))
                    tW.inc(ins)
                    k += 1

        @blk.vector
        def _(v):
            k = 0
            for q in range(4):
                for tt in range(NT):
                    par = k % 2
                    lxg = ld_xg0 if par == 0 else ld_xg1
                    stu = st_u0 if par == 0 else st_u1
                    VE.wait(tW, k + 1)
                    VE.wait(lxg, 32 * (k // 2 + 1))
                    if k >= 2:
                        VE.wait(stu, 16 * (k // 2))
                    VE.run("tensor_mul", r4_sb[:, :], ps_w[:, par, :],
                           g4_sb[:, par, :])
                    VE.run("tensor_add", o4_sb[:, par, :], r4_sb[:, :],
                           x4_sb[:, par, :], m=vP4)
                    k += 1

        @blk.scalar
        def _(s):
            k = 0
            for q in range(4):
                for tt in range(NT):
                    par = k % 2
                    stu = st_u0 if par == 0 else st_u1
                    SE.wait(vP4, k + 1)
                    SE.run("dma_start", out[tt][:, 1024 * q:1024 * (q + 1)],
                           o4_sb[:, par, :], m=stu, dma=True)
                    k += 1
            s.wait_ge(st_u0.h, st_u0.v)
            s.wait_ge(st_u1.h, st_u1.v)

        @blk.sync
        def _(sy):
            sy.wait_ge(st_u0.h, st_u0.v)
            sy.wait_ge(st_u1.h, st_u1.v)



def _prep_shared(wq, wk, wv, wo, router_w, A, B):
    f32 = np.float32
    wq_blob = np.ascontiguousarray(wq.reshape(H, 128, ND, 128).transpose(0, 3, 2, 1))
    wk_blob = np.ascontiguousarray(wk.reshape(KV, 128, ND, 128).transpose(0, 3, 2, 1))
    wv_rhs = np.ascontiguousarray(wv.T.reshape(ND, 128, KV * HD))
    wo_tm = np.ascontiguousarray(wo.T).reshape(32, 128, D)

    Apad = np.zeros((D, 128), dtype=f32)
    for e in range(E):
        Apad[:, 32 * e:32 * e + 16] = A[e].T
    Apad[:, 16:20] = router_w.T
    a_blobm = np.ascontiguousarray(Apad.reshape(ND, 128, 128).transpose(1, 0, 2))

    Bpad = np.zeros((128, D), dtype=f32)
    for e in range(E):
        Bpad[32 * e:32 * e + 16, :] = B[e].T

    inv = 1.0 / 10000.0 ** (np.arange(0, HD, 2, dtype=np.float64) / HD)
    pos = np.arange(S, dtype=np.float64)
    fr = pos[:, None] * inv[None, :]
    cos64 = np.cos(fr)
    sin64 = np.sin(fr)
    cos_full = np.concatenate([cos64, cos64], axis=1)
    sinS = np.concatenate([-sin64, sin64], axis=1)
    cos_T = np.ascontiguousarray(cos_full.T).astype(f32)
    sin_T = np.ascontiguousarray(sinS.T).astype(f32)

    kpos = np.arange(S)
    key_scale = np.where(kpos < SYS, 0.5, np.where(kpos < BOUND, 1.5, 1.0))
    ks = np.zeros((128, 17), dtype=f32)
    for j in range(16):
        ks[:, j] = C_INV * key_scale[128 * j:128 * (j + 1)]
    ks[:, 16] = C_INV
    ksr = np.broadcast_to(np.tile(key_scale[:640], KV)[None, :],
                          (128, KV * 640)).astype(f32)
    ksr = np.ascontiguousarray(ksr)

    dia = np.zeros((128, 256), dtype=f32)
    for a in range(128):
        dia[a, a:128] = 1.0
        dia[a, 128 + a:] = 1.0

    i128m = np.eye(128, dtype=f32)
    e4m = np.zeros((4, 128), dtype=f32)
    for e in range(E):
        e4m[e, 32 * e:32 * e + 16] = 1.0
    onesm = np.ones((1, 128), dtype=f32)
    cv = np.zeros((128, 8), dtype=f32)
    cv[:, CV_NEGHALF] = -0.5
    cv[:, CV_NEG1] = -1.0
    cv[:, CV_FOUR] = 4.0

    return dict(wq_blob=wq_blob, wk_blob=wk_blob, wv_rhs=wv_rhs, wo_t=wo_tm,
                a_blob=a_blobm, b_pad=Bpad, cos_t=cos_T, sin_t=sin_T, keysc=ks,
                diagm=dia, i128=i128m, e4=e4m, onesf=onesm, cvec=cv, ksraw=ksr)


def _prep_core(x2d):
    x_tok = np.ascontiguousarray(x2d).reshape(NT, 128, D)
    x_feat = np.ascontiguousarray(x2d.T.reshape(ND, 128, S).transpose(1, 0, 2))
    return dict(x_tok=x_tok, x_feat=x_feat)


def kernel(L0_attn_hidden_states, L0_attn_wq, L0_attn_wk, L0_attn_wv, L0_attn_wo,
           L0_ffn_router_w, L0_ffn_A, L0_ffn_B):
    x = np.asarray(L0_attn_hidden_states, dtype=np.float32)
    shared = _prep_shared(
        np.asarray(L0_attn_wq, dtype=np.float32),
        np.asarray(L0_attn_wk, dtype=np.float32),
        np.asarray(L0_attn_wv, dtype=np.float32),
        np.asarray(L0_attn_wo, dtype=np.float32),
        np.asarray(L0_ffn_router_w, dtype=np.float32),
        np.asarray(L0_ffn_A, dtype=np.float32),
        np.asarray(L0_ffn_B, dtype=np.float32))

    nc = build()
    in_maps = []
    for b in range(8):
        m = dict(shared)
        m.update(_prep_core(x[b]))
        in_maps.append(m)
    res = run_bass_kernel_spmd(nc, in_maps, core_ids=list(range(8)))
    outs = [np.asarray(r["out"]).reshape(S, D) for r in res.results]
    return np.stack(outs, axis=0)


# version 27
# speedup vs baseline: 1.0517x; 1.0084x over previous
import math
from contextlib import ExitStack

import numpy as np
import concourse.bass as bass
import concourse.mybir as mybir
from concourse.bass_utils import run_bass_kernel_spmd

F32 = mybir.dt.float32
BF16 = mybir.dt.bfloat16
AF = mybir.ActivationFunctionType
OP = mybir.AluOpType
AX = mybir.AxisListType

S = 2048
D = 4096
H = 32
KV = 8
HD = 128
ND = 32
NT = 16
HALF = 1024
E = 4
SYS = 35
IMG = 576
BOUND = SYS + IMG
C_INV = 1.0 / math.sqrt(HD)
EPS = 1e-5

CV_NEGHALF = 0
CV_NEG1 = 1
CV_FOUR = 2

PAIR_SCHED = []
for _pr in range(H // 2):
    for _qt in range(4):
        _bands = [(j, max(512 * _qt, 128 * j), 512 * (_qt + 1))
                  for j in range(4 * _qt + 4)]
        PAIR_SCHED.append((_pr, _qt, _bands))
BANDS_PER_PAIR = 4 + 8 + 12 + 16


class Ctr:

    def __init__(self, h):
        self.h = h
        self.v = 0

    def inc(self, instr, k=1):
        instr.then_inc(self.h, k)
        self.v += k
        return self.v


class Eng:

    def __init__(self, eng, selfc):
        self.e = eng
        self.ss = selfc
        self.last = None

    def wait(self, ctr, val):
        self.e.wait_ge(ctr.h, val)

    def chain(self):
        if self.last is not None:
            self.e.wait_ge(self.last[0].h, self.last[1])

    def run(self, name, *a, m=None, dma=False, **kw):
        self.chain()
        ins = getattr(self.e, name)(*a, **kw)
        if dma:
            m.inc(ins, 16)
        else:
            c = self.ss if m is None else m
            c.inc(ins, 1)
            self.last = (c, c.v)
        return ins


def build(debug=False):
    nc = bass.Bass("TRN2", target_bir_lowering=False)

    x_feat = nc.dram_tensor("x_feat", [128, ND, S], F32, kind="ExternalInput")
    x_tok = nc.dram_tensor("x_tok", [NT, 128, D], F32, kind="ExternalInput")
    wq_blob = nc.dram_tensor("wq_blob", [H, 128, ND, 128], F32, kind="ExternalInput")
    wk_blob = nc.dram_tensor("wk_blob", [KV, 128, ND, 128], F32, kind="ExternalInput")
    wv_rhs = nc.dram_tensor("wv_rhs", [ND, 128, KV * HD], F32, kind="ExternalInput")
    wo_t = nc.dram_tensor("wo_t", [32, 128, D], F32, kind="ExternalInput")
    a_blob = nc.dram_tensor("a_blob", [128, ND, 128], F32, kind="ExternalInput")
    b_pad = nc.dram_tensor("b_pad", [128, D], F32, kind="ExternalInput")
    cos_t = nc.dram_tensor("cos_t", [128, S], F32, kind="ExternalInput")
    sin_t = nc.dram_tensor("sin_t", [128, S], F32, kind="ExternalInput")
    keysc = nc.dram_tensor("keysc", [128, 17], F32, kind="ExternalInput")
    diagm = nc.dram_tensor("diagm", [128, 256], F32, kind="ExternalInput")
    i128 = nc.dram_tensor("i128", [128, 128], F32, kind="ExternalInput")
    e4 = nc.dram_tensor("e4", [4, 128], F32, kind="ExternalInput")
    onesf = nc.dram_tensor("onesf", [1, 128], F32, kind="ExternalInput")
    cvec = nc.dram_tensor("cvec", [128, 8], F32, kind="ExternalInput")
    ksraw = nc.dram_tensor("ksraw", [128, KV * 640], F32, kind="ExternalInput")
    out = nc.dram_tensor("out", [NT, 128, D], F32, kind="ExternalOutput")

    kind_stash = "ExternalOutput" if debug else "Internal"
    qstash = nc.dram_tensor("qstash", [H, 128, S], BF16, kind=kind_stash)
    kstash = nc.dram_tensor("kstash", [KV, 128, S], BF16, kind=kind_stash)
    vstash = nc.dram_tensor("vstash", [NT, 128, KV, HD], BF16, kind=kind_stash)
    gstash = nc.dram_tensor("gstash", [NT, 128, D], BF16, kind=kind_stash)
    ostash = nc.dram_tensor("ostash", [NT, 128, H, 128], BF16, kind=kind_stash)
    xdbg = None
    if debug:
        xdbg = nc.dram_tensor("xdbg", [2, 128, ND, HALF], BF16, kind="ExternalOutput")

    env = {"nc": nc, "debug": debug, "x_feat": x_feat, "x_tok": x_tok, "ksraw": ksraw,
           "cos_t": cos_t, "sin_t": sin_t,
           "wq_blob": wq_blob, "wk_blob": wk_blob, "wv_rhs": wv_rhs, "wo_t": wo_t,
           "out": out, "qstash": qstash, "kstash": kstash, "vstash": vstash,
           "gstash": gstash, "ostash": ostash, "xdbg": xdbg}

    with ExitStack() as ctx:
        def sb(name, shape, dt):
            t = ctx.enter_context(nc.sbuf_tensor(name, shape, dt))
            env[name] = t
            return t

        sb("keysc_sb", [128, 17], F32)
        sb("diag_sb", [128, 256], BF16)
        sb("i128_sb", [128, 128], F32)
        sb("e4_sb", [4, 128], BF16)
        sb("onesf_sb", [1, 128], F32)
        sb("ones_sb", [128, 128], BF16)
        sb("cvec_sb", [128, 8], F32)
        sb("bpad_sb", [128, D], BF16)
        sb("ab_sb", [128, ND, 128], BF16)

        def sem(name):
            c = Ctr(ctx.enter_context(nc.semaphore(name)))
            env[name] = c
            return c

        for n in ["ld_c", "init", "ld_x", "ld_x1", "ld_x2", "ld_x3", "vsq", "tsq", "stev", "vst", "sst",
                  "tbc", "vnorm", "tmid", "smid", "ld_lg", "tlt", "vga", "sge",
                  "vgt", "tgT", "sgT", "tgb", "vpg", "td", "ssil", "st_g0", "st_g1", "st_g2", "st_g3", "ld_w0", "ld_w1", "tpr", "sev", "ld_sw", "vrope",
                  "st_p0", "st_p1", "ld_v0", "ld_v1", "ld_v2", "ld_sw0", "ld_sw1", "tv", "svev", "st_v",
                  "ld_kv", "ld_ks", "vks", "ld_q0", "ld_q1", "tS", "sexp", "vdg", "tPV", "vON",
                  "st_o0", "st_o1", "st_o2", "st_o3", "sON0", "sON1", "vON0", "vON1", "ld_wo0", "ld_wo1", "ld_at0", "ld_at1", "ld_xg0",
                  "ld_xg1", "tW", "vP4", "st_u0", "st_u1", "st_dbg",
                  "sbc", "vself", "sself"]:
            sem(n)

        VE = Eng(nc.vector, env["vself"])
        SE = Eng(nc.scalar, env["sself"])
        env["VE"] = VE
        env["SE"] = SE

        ld_c, init = env["ld_c"], env["init"]

        env["const_pairs"] = [("keysc_sb", keysc), ("diag_sb", diagm),
                              ("i128_sb", i128), ("e4_sb", e4),
                              ("onesf_sb", onesf), ("cvec_sb", cvec),
                              ("bpad_sb", b_pad), ("ab_sb", a_blob)]
        with nc.Block() as blk:
            @blk.vector
            def _(v):
                VE.run("memset", env["ones_sb"][:, :], 1.0, m=init)

        e1 = ExitStack()
        with e1:
            def sb1(name, shape, dt):
                t = e1.enter_context(nc.sbuf_tensor(name, shape, dt))
                env[name] = t
                return t

            sb1("cos_sb", [128, S], F32)
            sb1("sin_sb", [128, S], F32)
            sb1("xn_sb", [128, ND, HALF], BF16)
            sb1("sq_sb", [128, 2, HALF], BF16)
            sb1("st_sb", [1, 3 * HALF], F32)
            sb1("wb_sb", [128, 2, ND, 128], BF16)
            sb1("wv_sb", [128, 3, 4, KV * HD], BF16)
            sb1("qe_sb", [128, 2, HALF], BF16)
            sb1("qs_sb", [128, 2, HALF], BF16)
            sb1("rt1_sb", [128, HALF], F32)
            sb1("rt2_sb", [128, HALF], F32)
            sb1("qr_sb", [128, 2, HALF], BF16)
            sb1("ve_sb", [128, 4, KV * HD], BF16)
            sb1("mid_sb", [128, HALF], F32)
            sb1("lg4_sb", [4, HALF], F32)
            sb1("gt_sb", [128, 32], F32)
            sb1("gTb_sb", [4, HALF], BF16)
            sb1("negm1_sb", [128, 8], F32)
            sb1("negm2_sb", [128, 8], F32)
            sb1("gtmp_sb", [128, 32], F32)
            sb1("keep_sb", [128, 32], F32)
            sb1("et_sb", [128, 32], F32)
            sb1("gs_sb", [128, 8], F32)
            sb1("grs_sb", [128, 8], F32)
            sb1("pg_sb", [128, HALF], BF16)
            sb1("gsil_sb", [128, 4, 1024], BF16)

            env["kpre_blob"] = wk_blob
            for half in range(2):
                with nc.Block() as blk0:
                    env["blk"] = blk0
                    _p0(half, env)
                    _ffn_mid(half, env)
                env.pop("blk", None)
                with (
                    nc.psum_tensor(f"ps_lt_{half}", [128, 32], F32) as ps_lt,
                    nc.Block() as blkg,
                ):
                    env["blk2"] = blkg
                    env["ps_lt"] = ps_lt
                    _ffn_gates_pre(half, env)
                    _proj_rope(half, KV, wk_blob, kstash, env, light=True,
                               single=True)
                env.pop("blk2", None)
                _ffn_gates_post(half, env)
                _preload_blobs(half, env, wq_blob, 2)
                _vproj(half, env)
                with nc.psum_tensor(f"ps_dl_{half}", [128, 2, 1024], F32) as ps_dl:
                    hk = _ffn_delta_hooks(half, env, ps_dl)
                    _proj_rope(half, H, wq_blob, qstash, env, light=False,
                               hooks=hk)

        with nc.Block() as blk:
            @blk.scalar
            def _(s):
                for n in ["st_p0", "st_p1", "st_v", "st_g0", "st_g1", "st_g2", "st_g3"]:
                    s.wait_ge(env[n].h, env[n].v)
                if debug:
                    s.wait_ge(env["st_dbg"].h, env["st_dbg"].v)
        nc.all_engine_barrier()

        env["wo_sb"] = ctx.enter_context(nc.sbuf_tensor("wo_sb", [128, 32, 1024], BF16))
        e2 = ExitStack()
        with e2:
            for name, shape, dt in [
                ("k_sb", [128, KV, S], BF16),
                ("kscal_sb", [128, KV, 640], BF16),
                ("v_sb", [128, NT, KV, HD], BF16),
                ("q_sb", [128, 2, 2, S], BF16),
                ("p_sb", [128, 8, 2, 512], BF16),
                ("rd_sb", [128, 2, 512], F32),
                ("o_sb", [128, 4, 512], BF16),
            ]:
                env[name] = e2.enter_context(nc.sbuf_tensor(name, shape, dt))
            _p3(env)

        with nc.Block() as blk:
            @blk.scalar
            def _(s):
                for n in ["st_o0", "st_o1", "st_o2", "st_o3"]:
                    s.wait_ge(env[n].h, env[n].v)
        nc.all_engine_barrier()

        e3 = ExitStack()
        with e3:
            for name, shape, dt in [
                ("wo2_sb", [128, 32, 1024], BF16),
                ("at_sb", [128, 2, H, 128], BF16),
                ("x4_sb", [128, 2, 1024], F32),
                ("g4_sb", [128, 2, 1024], BF16),
                ("o4_sb", [128, 2, 1024], F32),
                ("r4_sb", [128, 1024], F32),
            ]:
                env[name] = e3.enter_context(nc.sbuf_tensor(name, shape, dt))
            _p4(env)

    return nc


def _p0(half, env):
    nc = env["nc"]
    debug = env["debug"]
    VE, SE = env["VE"], env["SE"]
    (ld_x, vsq, tsq, stev, vst, sst, tbc, vnorm, ld_c, init, tpr, sev,
     st_dbg, sbc) = (env[n] for n in
                ["ld_x", "vsq", "tsq", "stev", "vst", "sst", "tbc", "vnorm",
                 "ld_c", "init", "tpr", "sev", "st_dbg", "sbc"])
    xn_sb, sq_sb, st_sb = (
        env[n] for n in ["xn_sb", "sq_sb", "st_sb"])
    tmp_sb = sq_sb
    ones_sb, onesf_sb, cvec_sb = env["ones_sb"], env["onesf_sb"], env["cvec_sb"]
    x_feat = env["x_feat"]
    t0 = half * HALF
    b_tsq = tsq.v
    b_vsq = vsq.v
    b_lx = [env[n].v for n in ["ld_x", "ld_x1", "ld_x2", "ld_x3"]]
    b_tpr = tpr.v
    b_sev = sev.v

    blk = env["blk"]
    with (
        nc.psum_tensor(f"ps_stat_{half}", [1, 2, HALF], F32) as ps_stat,
        nc.psum_tensor(f"ps_bc_{half}", [128, 2, HALF], F32) as ps_bc,
    ):
        @blk.gpsimd
        def _(g):
            if half == 1:
                g.wait_ge(tpr.h, b_tpr)
                g.wait_ge(env["tv"].h, env["tv"].v)
                g.wait_ge(env["tmid"].h, env["tmid"].v)
                g.wait_ge(tsq.h, b_tsq)
                g.wait_ge(vsq.h, b_vsq)
                if debug:
                    g.wait_ge(st_dbg.h, st_dbg.v)
            for q in range(4):
                lx = env[["ld_x", "ld_x1", "ld_x2", "ld_x3"][q]]
                g.dma_start(
                    xn_sb[:, 8 * q:8 * (q + 1), :],
                    x_feat[:, 8 * q:8 * (q + 1), t0:t0 + HALF],
                ).then_inc(lx.h, 16)
                lx.v += 16
            if half == 0:
                g.dma_start(env["cos_sb"][:, :], env["cos_t"][:, :]).then_inc(ld_c.h, 16)
                g.dma_start(env["sin_sb"][:, :], env["sin_t"][:, :]).then_inc(ld_c.h, 16)
                ld_c.v += 32
                for dn, srct in env["const_pairs"]:
                    dst = env[dn]
                    idx = tuple(slice(None) for _ in dst.shape)
                    sidx = tuple(slice(None) for _ in srct.shape)
                    g.dma_start(dst[idx], srct[sidx]).then_inc(ld_c.h, 16)
                    ld_c.v += 16
            kblob = env["kpre_blob"]
            env["qblob_base"] = [env["ld_w0"].v, env["ld_w1"].v]
            for f in range(2):
                lw = env["ld_w0"] if f % 2 == 0 else env["ld_w1"]
                g.dma_start(env["wb_sb"][:, f % 2, :, :],
                            kblob[f]).then_inc(lw.h, 16)
                lw.v += 16
            env["qblob_pre"] = 2
            wv_r = env["wv_rhs"].rearrange("c p f -> p c f")
            env["vblob_base"] = [env[n].v for n in ["ld_v0", "ld_v1", "ld_v2"]]
            for gi in range(3):
                lv = env[["ld_v0", "ld_v1", "ld_v2"][gi]]
                g.dma_start(env["wv_sb"][:, gi, :, :],
                            wv_r[:, 4 * gi:4 * (gi + 1), :]).then_inc(lv.h, 16)
                lv.v += 16
            env["vpre"] = 3

        @blk.vector
        def _(v):
            for c in range(ND):
                if c % 8 == 0:
                    lx = env[["ld_x", "ld_x1", "ld_x2", "ld_x3"][c // 8]]
                    VE.wait(lx, b_lx[c // 8] + 16)
                if c >= 2:
                    VE.wait(tsq, b_tsq + c - 1)
                VE.run("tensor_mul", sq_sb[:, c % 2, :], xn_sb[:, c, :],
                       xn_sb[:, c, :], m=vsq)

        @blk.tensor
        def _(t):
            t.wait_ge(init.h, 1)
            t.wait_ge(ld_c.h, ld_c.v)
            if half == 1:
                t.wait_ge(sev.h, b_sev)
            for c in range(ND):
                if c % 8 == 0:
                    lx = env[["ld_x", "ld_x1", "ld_x2", "ld_x3"][c // 8]]
                    t.wait_ge(lx.h, b_lx[c // 8] + 16)
                for b in range(2):
                    t.matmul(ps_stat[:, 0, 512 * b:512 * (b + 1)],
                             ones_sb[:, 0:1],
                             xn_sb[:, c, 512 * b:512 * (b + 1)],
                             start=(c == 0), stop=(c == ND - 1))
                t.wait_ge(vsq.h, b_vsq + c + 1)
                for b in range(2):
                    ins = t.matmul(ps_stat[:, 1, 512 * b:512 * (b + 1)],
                                   ones_sb[:, 0:1],
                                   sq_sb[:, c % 2, 512 * b:512 * (b + 1)],
                                   start=(c == 0), stop=(c == ND - 1))
                tsq.inc(ins)

        @blk.scalar
        def _(s):
            SE.wait(tsq, b_tsq + ND)
            SE.run("copy", st_sb[:, 0:HALF], ps_stat[:, 0, :])
            SE.run("copy", st_sb[:, HALF:2 * HALF], ps_stat[:, 1, :], m=stev)

        @blk.vector
        def _(v):
            VE.wait(stev, stev.v)
            VE.run("tensor_scalar_mul", st_sb[:, 0:HALF], st_sb[:, 0:HALF], 1.0 / D)
            VE.run("tensor_mul", st_sb[:, 2 * HALF:3 * HALF],
                   st_sb[:, 0:HALF], st_sb[:, 0:HALF])
            VE.run("tensor_scalar", st_sb[:, HALF:2 * HALF], st_sb[:, HALF:2 * HALF],
                   1.0 / D, EPS, OP.mult, OP.add)
            VE.run("tensor_sub", st_sb[:, 2 * HALF:3 * HALF],
                   st_sb[:, HALF:2 * HALF], st_sb[:, 2 * HALF:3 * HALF], m=vst)

        @blk.scalar
        def _(s):
            SE.wait(vst, vst.v)
            SE.run("activation", st_sb[:, 2 * HALF:3 * HALF],
                   st_sb[:, 2 * HALF:3 * HALF], AF.Ln)
            SE.run("activation", st_sb[:, HALF:2 * HALF],
                   st_sb[:, 2 * HALF:3 * HALF], AF.Exp,
                   scale=cvec_sb[0:1, CV_NEGHALF:CV_NEGHALF + 1], m=sst)

        @blk.tensor
        def _(t):
            t.wait_ge(sst.h, sst.v)
            for b in range(2):
                t.matmul(ps_bc[:, 0, 512 * b:512 * (b + 1)], onesf_sb[:, :],
                         st_sb[:, 512 * b:512 * (b + 1)],
                         start=True, stop=True)
                ins = t.matmul(ps_bc[:, 1, 512 * b:512 * (b + 1)], onesf_sb[:, :],
                               st_sb[:, HALF + 512 * b:HALF + 512 * (b + 1)],
                               start=True, stop=True)
            tbc.inc(ins)

        @blk.scalar
        def _(s):
            SE.wait(tbc, tbc.v)
            SE.run("copy", sq_sb[:, 0, :], ps_bc[:, 0, :])
            SE.run("copy", sq_sb[:, 1, :], ps_bc[:, 1, :], m=sbc)

        @blk.vector
        def _(v):
            VE.wait(sbc, sbc.v)
            for c in range(ND):
                VE.run("tensor_sub", xn_sb[:, c, :], xn_sb[:, c, :], sq_sb[:, 0, :])
                VE.run("tensor_mul", xn_sb[:, c, :], xn_sb[:, c, :], sq_sb[:, 1, :],
                       m=vnorm)

        if debug:
            @blk.scalar
            def _(s):
                SE.wait(vnorm, vnorm.v)
                SE.run("dma_start", env["xdbg"][half], xn_sb[:, :, :],
                       m=st_dbg, dma=True)


def _ffn_mid(half, env):
    nc = env["nc"]
    SE = env["SE"]
    vnorm, tmid, smid, ld_lg = (env[n] for n in ["vnorm", "tmid", "smid", "ld_lg"])
    xn_sb, ab_sb, mid_sb, lg4_sb = (env[n] for n in
                                    ["xn_sb", "ab_sb", "mid_sb", "lg4_sb"])

    blk = env["blk"]
    with (
        nc.psum_tensor(f"ps_mid_{half}", [128, HALF], F32) as ps_mid,
    ):
        @blk.tensor
        def _(t):
            b_vn = vnorm.v - ND
            for b in range(2):
                for c in range(ND):
                    if b == 0:
                        t.wait_ge(vnorm.h, b_vn + c + 1)
                    ins = t.matmul(ps_mid[:, 512 * b:512 * (b + 1)], ab_sb[:, c, :],
                                   xn_sb[:, c, 512 * b:512 * (b + 1)],
                                   start=(c == 0), stop=(c == ND - 1))
            tmid.inc(ins)

        @blk.scalar
        def _(s):
            SE.wait(tmid, tmid.v)
            SE.wait(env["ld_lg"], env["ld_lg"].v)
            SE.run("copy", mid_sb[:, :], ps_mid[:, :], m=smid)

        @blk.sync
        def _(sy):
            sy.wait_ge(smid.h, smid.v)
            sy.dma_start(lg4_sb[:, :], mid_sb[16:20, :]).then_inc(ld_lg.h, 16)
            ld_lg.v += 16


def _ffn_gates_pre(half, env):
    nc = env["nc"]
    VE, SE = env["VE"], env["SE"]
    (ld_lg, tlt, vga, sge, vgt) = (
        env[n] for n in ["ld_lg", "tlt", "vga", "sge", "vgt"])
    (lg4_sb, i128_sb, gt_sb, negm1_sb, negm2_sb, gtmp_sb, keep_sb, et_sb,
     gs_sb, grs_sb) = (
        env[n] for n in ["lg4_sb", "i128_sb", "gt_sb", "negm1_sb", "negm2_sb",
                         "gtmp_sb", "keep_sb", "et_sb", "gs_sb", "grs_sb"])

    blk = env["blk2"]
    ps_lt = env["ps_lt"]
    if True:
        @blk.tensor
        def _(t):
            t.wait_ge(ld_lg.h, ld_lg.v)
            for tt in range(8):
                ins = t.matmul(ps_lt[:, 4 * tt:4 * (tt + 1)],
                               lg4_sb[:, 128 * tt:128 * (tt + 1)],
                               i128_sb[0:4, 0:4], start=True, stop=True)
            tlt.inc(ins)

        @blk.vector
        def _(v):
            VE.wait(tlt, tlt.v)
            for tt in range(8):
                sl = ps_lt[:, 4 * tt:4 * (tt + 1)]
                VE.run("reduce_max", negm1_sb[:, tt:tt + 1], sl, AX.X, negate=True)
                VE.run("tensor_scalar", gtmp_sb[:, 4 * tt:4 * (tt + 1)], sl,
                       negm1_sb[:, tt:tt + 1], 0.0, OP.add, OP.is_ge)
                VE.run("scalar_tensor_tensor", gtmp_sb[:, 4 * tt:4 * (tt + 1)],
                       gtmp_sb[:, 4 * tt:4 * (tt + 1)], -1e30, sl,
                       OP.mult, OP.add)
                VE.run("reduce_max", negm2_sb[:, tt:tt + 1],
                       gtmp_sb[:, 4 * tt:4 * (tt + 1)], AX.X, negate=True)
                VE.run("tensor_scalar", keep_sb[:, 4 * tt:4 * (tt + 1)], sl,
                       negm2_sb[:, tt:tt + 1], 0.0, OP.add, OP.is_ge,
                       m=(vga if tt == 7 else None))

        @blk.scalar
        def _(s):
            SE.wait(vga, vga.v)
            for tt in range(8):
                SE.run("activation", et_sb[:, 4 * tt:4 * (tt + 1)],
                       ps_lt[:, 4 * tt:4 * (tt + 1)], AF.Exp,
                       bias=negm1_sb[:, tt:tt + 1],
                       m=(sge if tt == 7 else None))

        @blk.vector
        def _(v):
            VE.wait(sge, sge.v)
            VE.run("tensor_mul", et_sb[:, :], et_sb[:, :], keep_sb[:, :])
            for tt in range(8):
                VE.run("reduce_sum", gs_sb[:, tt:tt + 1],
                       et_sb[:, 4 * tt:4 * (tt + 1)], AX.X)
            VE.run("reciprocal", grs_sb[:, :], gs_sb[:, :])
            for tt in range(8):
                VE.run("tensor_scalar_mul", gt_sb[:, 4 * tt:4 * (tt + 1)],
                       et_sb[:, 4 * tt:4 * (tt + 1)], grs_sb[:, tt:tt + 1],
                       m=(vgt if tt == 7 else None))


def _ffn_gates_post(half, env):
    nc = env["nc"]
    VE, SE = env["VE"], env["SE"]
    (vnorm, vgt, tgT, sgT, tgb, vpg, sev) = (
        env[n] for n in ["vnorm", "vgt", "tgT", "sgT", "tgb", "vpg", "sev"])
    gt_sb, gTb_sb, i128_sb, e4_sb, pg_sb, mid_sb = (
        env[n] for n in ["gt_sb", "gTb_sb", "i128_sb", "e4_sb", "pg_sb",
                         "mid_sb"])

    with (
        nc.psum_tensor(f"ps_gT_{half}", [4, HALF], F32) as ps_gT,
        nc.Block() as blk,
    ):
        @blk.tensor
        def _(t):
            t.wait_ge(vgt.h, vgt.v)
            t.wait_ge(sev.h, sev.v)
            for tt in range(8):
                ins = t.matmul(ps_gT[:, 128 * tt:128 * (tt + 1)],
                               gt_sb[:, 4 * tt:4 * (tt + 1)], i128_sb[:, :],
                               start=True, stop=True)
            tgT.inc(ins)

        @blk.scalar
        def _(s):
            SE.wait(tgT, tgT.v)
            SE.run("copy", gTb_sb[:, :], ps_gT[:, :], m=sgT)

    with (
        nc.psum_tensor(f"ps_gb_{half}", [128, HALF], F32) as ps_gb,
        nc.Block() as blk,
    ):
        @blk.tensor
        def _(t):
            t.wait_ge(sgT.h, sgT.v)
            t.wait_ge(vnorm.h, vnorm.v)
            for b in range(2):
                ins = t.matmul(ps_gb[:, 512 * b:512 * (b + 1)], e4_sb[:, :],
                               gTb_sb[:, 512 * b:512 * (b + 1)], start=True, stop=True)
            tgb.inc(ins)

        @blk.vector
        def _(v):
            VE.wait(tgb, tgb.v)
            VE.run("tensor_mul", pg_sb[:, :], mid_sb[:, :], ps_gb[:, :], m=vpg)


def _ffn_delta_hooks(half, env, ps_dl):
    SE = env["SE"]
    (vpg, td, ssil, svev, sev) = (env[n] for n in
                                  ["vpg", "td", "ssil", "svev", "sev"])
    st_g = [env[f"st_g{i}"] for i in range(4)]
    pg_sb, gsil_sb, bpad_sb, cvec_sb = (
        env[n] for n in ["pg_sb", "gsil_sb", "bpad_sb", "cvec_sb"])
    gstash = env["gstash"]
    b_td = td.v
    b_sil = ssil.v
    b_g = [c.v for c in st_g]

    def t_hook(t, f):
        if f >= 32:
            return
        if f == 0:
            t.wait_ge(vpg.h, vpg.v)
            t.wait_ge(svev.h, svev.v)
            t.wait_ge(sev.h, sev.v)
        for k in (f,):
            tt, o2 = k // 4, k % 4
            if k >= 2:
                t.wait_ge(ssil.h, b_sil + k - 1)
            for b in range(2):
                ins = t.matmul(ps_dl[:, k % 2, 512 * b:512 * (b + 1)],
                               pg_sb[:, 128 * tt:128 * (tt + 1)],
                               bpad_sb[:, 1024 * o2 + 512 * b:
                                       1024 * o2 + 512 * (b + 1)],
                               start=True, stop=True)
            td.inc(ins)

    def s_hook(s, f):
        if f >= 32:
            return
        for k in (f,):
            tt, o2 = k // 4, k % 4
            stg = st_g[k % 4]
            SE.wait(td, b_td + k + 1)
            SE.wait(stg, b_g[k % 4] + 16 * (k // 4))
            SE.run("activation", gsil_sb[:, k % 4, :], ps_dl[:, k % 2, :],
                   AF.Silu, scale=cvec_sb[:, CV_FOUR:CV_FOUR + 1], m=ssil)
            SE.run("dma_start",
                   gstash[half * 8 + tt][:, 1024 * o2:1024 * (o2 + 1)],
                   gsil_sb[:, k % 4, :], m=stg, dma=True)

    return {"tensor": t_hook, "scalar": s_hook}


def _proj_rope(half, NF, blob, stash, env, light=False, single=False, hooks=None):
    nc = env["nc"]
    VE, SE = env["VE"], env["SE"]
    (ld_w0, ld_w1, tpr, sev, ld_sw0, ld_sw1, vrope, st_p0, st_p1, vnorm,
     ld_c) = (
        env[n] for n in ["ld_w0", "ld_w1", "tpr", "sev", "ld_sw0", "ld_sw1",
                         "vrope", "st_p0", "st_p1", "vnorm", "ld_c"])
    (wb_sb, qe_sb, qs_sb, rt1_sb, rt2_sb, qr_sb, xn_sb, cos_sb, sin_sb) = (
        env[n] for n in ["wb_sb", "qe_sb", "qs_sb", "rt1_sb", "rt2_sb",
                         "qr_sb", "xn_sb", "cos_sb", "sin_sb"])
    t0 = half * HALF
    pre = env.pop("qblob_pre", 0)
    if pre:
        b_w = env.pop("qblob_base")
    else:
        b_w = [ld_w0.v, ld_w1.v]
    b_tpr = tpr.v
    b_sev = sev.v
    b_sw = [ld_sw0.v, ld_sw1.v]
    b_rp = vrope.v
    b_p = [st_p0.v, st_p1.v]

    es_p = ExitStack()
    with es_p:
        blk = env.pop("blk2", None)
        if blk is None:
            blk = es_p.enter_context(nc.Block())
        ps_pa = es_p.enter_context(
            nc.psum_tensor(f"ps_pa_{half}_{NF}", [128, HALF], F32))
        if single:
            ps_p = [ps_pa, ps_pa]
        else:
            ps_pb = es_p.enter_context(
                nc.psum_tensor(f"ps_pb_{half}_{NF}", [128, HALF], F32))
            ps_p = [ps_pa, ps_pb]
        @blk.gpsimd
        def _(g):
            def load_blob(f):
                par = f % 2
                lw = ld_w0 if par == 0 else ld_w1
                g.wait_ge(tpr.h, b_tpr + max(0, f - 1))
                g.dma_start(wb_sb[:, par, :, :], blob[f]).then_inc(lw.h, 16)
                lw.v += 16

            for f in range(pre, NF):
                load_blob(f)

        @blk.sync
        def _(sy):
            for f in range(NF):
                par = f % 2
                lsw = ld_sw0 if par == 0 else ld_sw1
                sy.wait_ge(sev.h, b_sev + f + 1)
                sy.wait_ge(vrope.h, b_rp + max(0, f - 1))
                sy.dma_start(qs_sb[0:64, par, :],
                             qe_sb[64:128, par, :]).then_inc(lsw.h, 16)
                sy.dma_start(qs_sb[64:128, par, :],
                             qe_sb[0:64, par, :]).then_inc(lsw.h, 16)
                lsw.v += 32

        @blk.tensor
        def _(t):
            t.wait_ge(vnorm.h, vnorm.v)
            if light:
                t.wait_ge(env["smid"].h, env["smid"].v)
            else:
                t.wait_ge(env["vpg"].h, env["vpg"].v)
            t.wait_ge(env["svev"].h, env["svev"].v)
            t.wait_ge(sev.h, b_sev)
            for f in range(NF):
                par = f % 2
                lw = ld_w0 if par == 0 else ld_w1
                t.wait_ge(lw.h, b_w[par] + 16 * (f // 2 + 1))
                if single:
                    if f >= 1:
                        t.wait_ge(sev.h, b_sev + f)
                elif f >= 2:
                    t.wait_ge(sev.h, b_sev + f - 1)
                for c in range(ND):
                    for b in range(2):
                        ins = t.matmul(ps_p[0 if single else par][:, 512 * b:512 * (b + 1)],
                                       wb_sb[:, par, c, :],
                                       xn_sb[:, c, 512 * b:512 * (b + 1)],
                                       start=(c == 0), stop=(c == ND - 1))
                tpr.inc(ins)
                if hooks is not None:
                    hooks["tensor"](t, f)

        @blk.scalar
        def _(s):
            for f in range(NF + 1):
                if f < NF:
                    par = f % 2
                    SE.wait(tpr, b_tpr + f + 1)
                    lsw = ld_sw0 if par == 0 else ld_sw1
                    SE.wait(lsw, b_sw[par] + 32 * (f // 2))
                    SE.wait(vrope, b_rp + max(0, f - 1))
                    SE.run("copy", qe_sb[:, par, :], ps_p[0 if single else par][:, :], m=sev)
                if f >= 1:
                    fp = f - 1
                    par = fp % 2
                    stp = st_p0 if par == 0 else st_p1
                    SE.wait(vrope, b_rp + fp + 1)
                    SE.run("dma_start", stash[fp][:, t0:t0 + HALF],
                           qr_sb[:, par, :], m=stp, dma=True)
                if hooks is not None and f < NF:
                    hooks["scalar"](s, f)

        @blk.vector
        def _(v):
            VE.wait(ld_c, ld_c.v)
            for f in range(NF):
                par = f % 2
                stp = st_p0 if par == 0 else st_p1
                lsw = ld_sw0 if par == 0 else ld_sw1
                VE.wait(lsw, b_sw[par] + 32 * (f // 2 + 1))
                VE.wait(stp, b_p[par] + 16 * (f // 2))
                VE.run("tensor_mul", rt1_sb[:, :], qe_sb[:, par, :],
                       cos_sb[:, t0:t0 + HALF])
                VE.run("tensor_mul", rt2_sb[:, :], qs_sb[:, par, :],
                       sin_sb[:, t0:t0 + HALF])
                VE.run("tensor_add", qr_sb[:, par, :], rt1_sb[:, :], rt2_sb[:, :],
                       m=vrope)


def _preload_blobs(half, env, blob, n):
    nc = env["nc"]
    ld_w0, ld_w1, tpr = env["ld_w0"], env["ld_w1"], env["tpr"]
    env["qblob_base"] = [ld_w0.v, ld_w1.v]
    wb_sb = env["wb_sb"]
    with nc.Block() as blk:
        @blk.gpsimd
        def _(g):
            for f in range(n):
                par = f % 2
                lw = ld_w0 if par == 0 else ld_w1
                g.wait_ge(tpr.h, tpr.v)
                g.dma_start(wb_sb[:, par, :, :], blob[f]).then_inc(lw.h, 16)
                lw.v += 16
    env["qblob_pre"] = n


def _vproj(half, env):
    nc = env["nc"]
    SE = env["SE"]
    ld_v0, ld_v1, ld_v2, tv, svev, st_v, vnorm = (
        env[n] for n in ["ld_v0", "ld_v1", "ld_v2", "tv", "svev", "st_v",
                         "vnorm"])
    lvs = [ld_v0, ld_v1, ld_v2]
    wv_sb, ve_sb, xn_sb = env["wv_sb"], env["ve_sb"], env["xn_sb"]
    wv_rhs, vstash = env["wv_rhs"], env["vstash"]
    pre_v = env.pop("vpre", 0)
    if pre_v:
        b_v = env.pop("vblob_base")
    else:
        b_v = [ld_v0.v, ld_v1.v, ld_v2.v]
    b_tv = tv.v
    b_sv = svev.v
    b_stv = st_v.v

    with (
        nc.psum_tensor(f"ps_v_{half}", [128, 4, KV * HD], F32) as ps_v,
        nc.Block() as blk,
    ):
        @blk.gpsimd
        def _(g):
            for gi in range(pre_v, 2 * ND // 4):
                par = gi % 3
                lv = lvs[par]
                cc = (gi * 4) % ND
                g.wait_ge(tv.h, b_tv + max(0, 4 * (gi - 2)))
                g.dma_start(wv_sb[:, par, :, :],
                            wv_rhs.rearrange("c p f -> p c f")[:, cc:cc + 4, :]
                            ).then_inc(lv.h, 16)
                lv.v += 16

        @blk.tensor
        def _(t):
            t.wait_ge(vnorm.h, vnorm.v)
            t.wait_ge(env["vpg"].h, env["vpg"].v)
            t.wait_ge(env["sev"].h, env["sev"].v)
            t.wait_ge(svev.h, b_sv)
            for grp in range(2):
                if grp == 1:
                    t.wait_ge(svev.h, b_sv + 4)
                for c in range(ND):
                    k = grp * ND + c
                    gi = k // 4
                    par = gi % 3
                    lv = lvs[par]
                    t.wait_ge(lv.h, b_v[par] + 16 * (gi // 3 + 1))
                    for tl in range(4):
                        tt = grp * 4 + tl
                        for b in range(2):
                            ins = t.matmul(ps_v[:, tl, 512 * b:512 * (b + 1)],
                                           xn_sb[:, c, 128 * tt:128 * (tt + 1)],
                                           wv_sb[:, par, c % 4, 512 * b:512 * (b + 1)],
                                           start=(c == 0), stop=(c == ND - 1))
                    tv.inc(ins)

        @blk.scalar
        def _(s):
            for grp in range(2):
                SE.wait(tv, b_tv + ND * (grp + 1))
                SE.wait(st_v, b_stv + (64 if grp == 1 else 0))
                for tl in range(4):
                    tt = grp * 4 + tl
                    SE.run("copy", ve_sb[:, tl, :], ps_v[:, tl, :], m=svev)
                    SE.run("dma_start", vstash[half * 8 + tt], ve_sb[:, tl, :],
                           m=st_v, dma=True)


def _p3(env):
    nc = env["nc"]
    VE, SE = env["VE"], env["SE"]
    (ld_kv, ld_q0, ld_q1, tS, sexp, vdg, tPV, ld_wo0, ld_c, init) = (
        env[n] for n in ["ld_kv", "ld_q0", "ld_q1", "tS", "sexp", "vdg", "tPV",
                         "ld_wo0", "ld_c", "init"])
    sON = [env["sON0"], env["sON1"]]
    vON = [env["vON0"], env["vON1"]]
    st_o = [env["st_o0"], env["st_o1"], env["st_o2"], env["st_o3"]]
    k_sb, kscal_sb, v_sb, q_sb, p_sb, rd_sb, o_sb = (
        env[n] for n in ["k_sb", "kscal_sb", "v_sb", "q_sb", "p_sb", "rd_sb",
                         "o_sb"])
    keysc_sb, diag_sb, ones_sb, wo_sb, cvec_sb = (
        env[n] for n in ["keysc_sb", "diag_sb", "ones_sb", "wo_sb", "cvec_sb"])
    kstash, vstash, qstash, ostash, wo_t = (
        env[n] for n in ["kstash", "vstash", "qstash", "ostash", "wo_t"])
    ksraw_r = env["ksraw"]

    flat = []
    pb = 0
    for (pr, qt, bands) in PAIR_SCHED:
        lst = []
        for (j, lo, hi) in bands:
            lst.append((pb, j, lo, hi))
            pb += 1
        flat.append((pr, qt, lst))
    n_band = pb
    dplan = {}
    dc = 0
    for (pr, qt, lst) in flat:
        for (g, j, lo, hi) in lst:
            if j >= 4 * qt:
                dc += 1
            dplan[g] = dc

    pv_order = []
    for (pr, qt, lst) in flat:
        seg = [(e, pr, qt) for e in lst]
        pv_order += [seg[1], seg[2], seg[0]] + seg[3:]
    env["pv_order"] = pv_order
    pvdone = {}
    for pos, (e, pr, qt) in enumerate(pv_order):
        pvdone[e[0]] = 2 * (pos + 1)
    gslot = {}
    for si, (pr, qt, lst) in enumerate(flat):
        for (g, j, lo, hi) in lst:
            gslot[g] = (6 + si % 2) if j == 0 else g % 6
    env["gslot"] = gslot
    slot_wait = {}
    last_user = {}
    for (pr, qt, lst) in flat:
        for (g, j, lo, hi) in lst:
            slot = gslot[g]
            if slot in last_user:
                slot_wait[g] = pvdone[last_user[slot]]
            last_user[slot] = g
    env["slot_wait"] = slot_wait

    with (
        nc.psum_tensor("ps_S", [128, 2, 2, 512], F32) as ps_S,
        nc.psum_tensor("ps_o", [128, 2, 512], F32) as ps_o,
        nc.psum_tensor("ps_d3", [128, 2, 512], F32) as ps_d3,
        nc.Block() as blk,
    ):
        @blk.gpsimd
        def _(g):
            g.dma_start(kscal_sb[:, :, :].rearrange("p k t -> p (k t)"),
                        ksraw_r[:, :]).then_inc(env["ld_ks"].h, 16)
            env["ld_ks"].v += 16
            for hc in range(32):
                g.dma_start(wo_sb[:, hc, :],
                            wo_t[hc][:, 0:1024]).then_inc(ld_wo0.h, 16)
                ld_wo0.v += 16

        @blk.sync
        def _(sy):
            sy.dma_start(k_sb[:, :, :],
                         kstash.rearrange("f p t -> p f t")).then_inc(ld_kv.h, 16)
            sy.dma_start(v_sb[:, :, :, :],
                         vstash.rearrange("t p k h -> p t k h")).then_inc(ld_kv.h, 16)
            ld_kv.v += 32
            for pr in range(H // 2):
                lq = ld_q0 if pr % 2 == 0 else ld_q1
                if pr >= 2:
                    sy.wait_ge(tS.h, BANDS_PER_PAIR * (pr - 1))
                for X in range(2):
                    sy.dma_start(q_sb[:, pr % 2, X, :],
                                 qstash[2 * pr + X]).then_inc(lq.h, 16)
                    lq.v += 16
                if pr >= 1:
                    for qt in range(4):
                        _ostash_write(sy, env, pr - 1, qt, vON, st_o, o_sb, ostash)
            for qt in range(4):
                _ostash_write(sy, env, H // 2 - 1, qt, vON, st_o, o_sb, ostash)

        @blk.tensor
        def _(t):
            t.wait_ge(ld_kv.h, 32)
            t.wait_ge(init.h, 1)
            t.wait_ge(env["vks"].h, 1)

            def emit_S(g, j, lo, hi, pr, qt):
                kvh = pr // 2
                L = hi - lo
                sp = g % 2
                if g >= 2:
                    t.wait_ge(sexp.h, g - 1)
                for X in range(2):
                    if qt == 1 and j <= 4:
                        bnd = BOUND - lo
                        t.matmul(ps_S[:, sp, X, 0:bnd],
                                 k_sb[:, kvh, 128 * j:128 * (j + 1)],
                                 q_sb[:, pr % 2, X, lo:lo + bnd],
                                 start=True, stop=False)
                        ins = t.matmul(ps_S[:, sp, X, bnd:L],
                                       kscal_sb[:, kvh, 128 * j:128 * (j + 1)],
                                       q_sb[:, pr % 2, X, lo + bnd:hi],
                                       start=False, stop=True)
                    else:
                        kk = kscal_sb if (qt >= 2 and j <= 4) else k_sb
                        ins = t.matmul(ps_S[:, sp, X, 0:L],
                                       kk[:, kvh, 128 * j:128 * (j + 1)],
                                       q_sb[:, pr % 2, X, lo:lo + L],
                                       start=True, stop=True)
                tS.inc(ins)

            def emit_PV(g, j, lo, hi, pr, qt):
                kvh = pr // 2
                L = hi - lo
                pp = env["gslot"][g]
                jlast = 4 * qt + 3
                if j >= 4 * qt:
                    t.wait_ge(vdg.h, dplan[g])
                else:
                    t.wait_ge(sexp.h, g + 1)
                segidx = 4 * pr + qt
                q0 = lo - 512 * qt
                first = (j == 1)
                if j == 0:
                    parts = [(0, 128), (128, 512)]
                else:
                    parts = [(q0, 512)]
                for X in range(2):
                    if first and segidx >= 1:
                        t.wait_ge(vON[X].h, segidx)
                    for (z0, z1) in parts:
                        t.matmul(ps_o[:, X, z0:z1], v_sb[:, j, kvh, :],
                                 p_sb[:, pp, X, z0 - q0:z1 - q0],
                                 start=first, stop=(j == jlast))
                        ins = t.matmul(ps_d3[:, X, z0:z1], ones_sb[:, :],
                                       p_sb[:, pp, X, z0 - q0:z1 - q0],
                                       start=first, stop=(j == jlast))
                    tPV.inc(ins)

            LAG = 4
            allb = []
            pv_order = env["pv_order"]
            for (pr, qt, lst) in flat:
                for e in lst:
                    allb.append((e, pr, qt))
            for i, (e, pr, qt) in enumerate(allb):
                if qt == 0 and e[1] == 0:
                    lq = ld_q0 if pr % 2 == 0 else ld_q1
                    t.wait_ge(lq.h, 32 * (pr // 2 + 1))
                emit_S(*e, pr, qt)
                if i >= LAG:
                    (e2, pr2, qt2) = pv_order[i - LAG]
                    emit_PV(*e2, pr2, qt2)
            for i in range(len(allb) - LAG, len(allb)):
                (e2, pr2, qt2) = pv_order[i]
                emit_PV(*e2, pr2, qt2)

        @blk.scalar
        def _(s):
            SE.wait(ld_c, ld_c.v)
            pend = None

            def emit_rden(last_pb):
                SE.wait(tPV, 2 * (last_pb + 1))
                SE.run("activation", rd_sb[:, :, :], ps_d3[:, :, :], AF.Ln)
                SE.run("activation", rd_sb[:, :, :], rd_sb[:, :, :], AF.Exp,
                       scale=cvec_sb[:, CV_NEG1:CV_NEG1 + 1], m=env["sON0"])

            for (pr, qt, lst) in flat:
                for bi, (g, j, lo, hi) in enumerate(lst):
                    L = hi - lo
                    sp = g % 2
                    SE.wait(tS, g + 1)
                    if g in env["slot_wait"]:
                        SE.wait(tPV, env["slot_wait"][g])
                    SE.run("activation",
                           p_sb[:, env["gslot"][g], :, 0:L],
                           ps_S[:, sp, :, 0:L],
                           AF.Exp, scale=keysc_sb[:, 16:17], m=sexp)
                    if bi == 1 and pend is not None:
                        emit_rden(pend)
                        pend = None
                pend = lst[-1][0]
            emit_rden(pend)
        @blk.vector
        def _(v):
            VE.wait(ld_c, ld_c.v)
            VE.wait(ld_kv, 32)
            VE.wait(env["ld_ks"], 16)
            for kv in range(KV):
                VE.run("tensor_mul", kscal_sb[:, kv, :], k_sb[:, kv, 0:640],
                       kscal_sb[:, kv, :],
                       m=(env["vks"] if kv == KV - 1 else None))
            for (pr, qt, lst) in flat:
                segidx = 4 * pr + qt
                for (g, j, lo, hi) in lst:
                    if j >= 4 * qt:
                        VE.wait(sexp, g + 1)
                        pp = env["gslot"][g]
                        VE.run("tensor_mul", p_sb[:, pp, :, 0:128],
                               p_sb[:, pp, :, 0:128],
                               diag_sb[:, :].rearrange("p (a x) -> p a x", a=2),
                               m=vdg)
                for X in range(2):
                    sgl = 2 * segidx + X
                    VE.wait(env["sON0"], segidx + 1)
                    VE.wait(st_o[sgl % 4], 16 * (sgl // 4))
                    VE.run("tensor_mul", o_sb[:, sgl % 4, :], ps_o[:, X, :],
                           rd_sb[:, X, :], m=vON[X])


def _ostash_write(sy, env, pr, qt, vON, st_o, o_sb, ostash):
    segidx = 4 * pr + qt
    for X in range(2):
        sgl = 2 * segidx + X
        sy.wait_ge(vON[X].h, segidx + 1)
        sto = st_o[sgl % 4]
        sy.dma_start(
            ostash.rearrange("t p h x -> p t h x")[:, 4 * qt:4 * qt + 4,
                                                   2 * pr + X, :],
            o_sb[:, sgl % 4, :].rearrange("p (t x) -> p t x", t=4),
        ).then_inc(sto.h, 16)
        sto.v += 16


def _p4(env):
    nc = env["nc"]
    VE, SE = env["VE"], env["SE"]
    (ld_wo0, ld_wo1, ld_at0, ld_at1, ld_xg0, ld_xg1, tW, vP4, st_u0,
     st_u1) = (env[n] for n in
               ["ld_wo0", "ld_wo1", "ld_at0", "ld_at1", "ld_xg0",
                "ld_xg1", "tW", "vP4", "st_u0", "st_u1"])
    wo_sb, wo2_sb, at_sb, x4_sb, g4_sb, o4_sb, r4_sb = (
        env[n] for n in ["wo_sb", "wo2_sb", "at_sb", "x4_sb", "g4_sb",
                         "o4_sb", "r4_sb"])
    wo_t, ostash, x_tok, gstash, out = (
        env[n] for n in ["wo_t", "ostash", "x_tok", "gstash", "out"])

    with (
        nc.psum_tensor("ps_w", [128, 2, 1024], F32) as ps_w,
        nc.Block() as blk,
    ):
        @blk.gpsimd
        def _(g):
            for q in range(1, 4):
                buf = wo_sb if q % 2 == 0 else wo2_sb
                lwo = ld_wo0 if q % 2 == 0 else ld_wo1
                g.wait_ge(tW.h, 16 * (q - 1))
                for hc in range(32):
                    g.dma_start(buf[:, hc, :],
                                wo_t[hc][:, 1024 * q:1024 * (q + 1)]
                                ).then_inc(lwo.h, 16)
                    lwo.v += 16

        @blk.sync
        def _(sy):
            for i in range(4):
                sy.wait_ge(env[f"st_g{i}"].h, env[f"st_g{i}"].v)
            k = 0
            for q in range(4):
                for tt in range(NT):
                    par = k % 2
                    lat = ld_at0 if par == 0 else ld_at1
                    lxg = ld_xg0 if par == 0 else ld_xg1
                    if k >= 2:
                        sy.wait_ge(tW.h, k - 1)
                        sy.wait_ge(vP4.h, k - 1)
                    sy.dma_start(at_sb[:, par, :, :], ostash[tt]).then_inc(lat.h, 16)
                    sy.dma_start(x4_sb[:, par, :],
                                 x_tok[tt][:, 1024 * q:1024 * (q + 1)]
                                 ).then_inc(lxg.h, 16)
                    sy.dma_start(g4_sb[:, par, :],
                                 gstash[tt][:, 1024 * q:1024 * (q + 1)]
                                 ).then_inc(lxg.h, 16)
                    lat.v += 16
                    lxg.v += 32
                    k += 1

        @blk.tensor
        def _(t):
            k = 0
            for q in range(4):
                lwo = ld_wo0 if q % 2 == 0 else ld_wo1
                t.wait_ge(lwo.h, 512 * (q // 2 + 1))
                buf = wo_sb if q % 2 == 0 else wo2_sb
                for tt in range(NT):
                    par = k % 2
                    lat = ld_at0 if par == 0 else ld_at1
                    t.wait_ge(lat.h, 16 * (k // 2 + 1))
                    if k >= 2:
                        t.wait_ge(vP4.h, k - 1)
                    for hc in range(32):
                        for b in range(2):
                            ins = t.matmul(ps_w[:, par, 512 * b:512 * (b + 1)],
                                           at_sb[:, par, hc, :],
                                           buf[:, hc, 512 * b:512 * (b + 1)],
                                           start=(hc == 0), stop=(hc == 31))
                    tW.inc(ins)
                    k += 1

        @blk.vector
        def _(v):
            k = 0
            for q in range(4):
                for tt in range(NT):
                    par = k % 2
                    lxg = ld_xg0 if par == 0 else ld_xg1
                    stu = st_u0 if par == 0 else st_u1
                    VE.wait(tW, k + 1)
                    VE.wait(lxg, 32 * (k // 2 + 1))
                    if k >= 2:
                        VE.wait(stu, 16 * (k // 2))
                    VE.run("tensor_mul", r4_sb[:, :], ps_w[:, par, :],
                           g4_sb[:, par, :])
                    VE.run("tensor_add", o4_sb[:, par, :], r4_sb[:, :],
                           x4_sb[:, par, :], m=vP4)
                    k += 1

        @blk.scalar
        def _(s):
            k = 0
            for q in range(4):
                for tt in range(NT):
                    par = k % 2
                    stu = st_u0 if par == 0 else st_u1
                    SE.wait(vP4, k + 1)
                    SE.run("dma_start", out[tt][:, 1024 * q:1024 * (q + 1)],
                           o4_sb[:, par, :], m=stu, dma=True)
                    k += 1
            s.wait_ge(st_u0.h, st_u0.v)
            s.wait_ge(st_u1.h, st_u1.v)

        @blk.sync
        def _(sy):
            sy.wait_ge(st_u0.h, st_u0.v)
            sy.wait_ge(st_u1.h, st_u1.v)



def _prep_shared(wq, wk, wv, wo, router_w, A, B):
    f32 = np.float32
    wq_blob = np.ascontiguousarray(wq.reshape(H, 128, ND, 128).transpose(0, 3, 2, 1))
    wk_blob = np.ascontiguousarray(wk.reshape(KV, 128, ND, 128).transpose(0, 3, 2, 1))
    wv_rhs = np.ascontiguousarray(wv.T.reshape(ND, 128, KV * HD))
    wo_tm = np.ascontiguousarray(wo.T).reshape(32, 128, D)

    Apad = np.zeros((D, 128), dtype=f32)
    for e in range(E):
        Apad[:, 32 * e:32 * e + 16] = A[e].T
    Apad[:, 16:20] = router_w.T
    a_blobm = np.ascontiguousarray(Apad.reshape(ND, 128, 128).transpose(1, 0, 2))

    Bpad = np.zeros((128, D), dtype=f32)
    for e in range(E):
        Bpad[32 * e:32 * e + 16, :] = B[e].T

    inv = 1.0 / 10000.0 ** (np.arange(0, HD, 2, dtype=np.float64) / HD)
    pos = np.arange(S, dtype=np.float64)
    fr = pos[:, None] * inv[None, :]
    cos64 = np.cos(fr)
    sin64 = np.sin(fr)
    cos_full = np.concatenate([cos64, cos64], axis=1)
    sinS = np.concatenate([-sin64, sin64], axis=1)
    cos_T = np.ascontiguousarray(cos_full.T).astype(f32)
    sin_T = np.ascontiguousarray(sinS.T).astype(f32)

    kpos = np.arange(S)
    key_scale = np.where(kpos < SYS, 0.5, np.where(kpos < BOUND, 1.5, 1.0))
    ks = np.zeros((128, 17), dtype=f32)
    for j in range(16):
        ks[:, j] = C_INV * key_scale[128 * j:128 * (j + 1)]
    ks[:, 16] = C_INV
    ksr = np.broadcast_to(np.tile(key_scale[:640], KV)[None, :],
                          (128, KV * 640)).astype(f32)
    ksr = np.ascontiguousarray(ksr)

    dia = np.zeros((128, 256), dtype=f32)
    for a in range(128):
        dia[a, a:128] = 1.0
        dia[a, 128 + a:] = 1.0

    i128m = np.eye(128, dtype=f32)
    e4m = np.zeros((4, 128), dtype=f32)
    for e in range(E):
        e4m[e, 32 * e:32 * e + 16] = 1.0
    onesm = np.ones((1, 128), dtype=f32)
    cv = np.zeros((128, 8), dtype=f32)
    cv[:, CV_NEGHALF] = -0.5
    cv[:, CV_NEG1] = -1.0
    cv[:, CV_FOUR] = 4.0

    return dict(wq_blob=wq_blob, wk_blob=wk_blob, wv_rhs=wv_rhs, wo_t=wo_tm,
                a_blob=a_blobm, b_pad=Bpad, cos_t=cos_T, sin_t=sin_T, keysc=ks,
                diagm=dia, i128=i128m, e4=e4m, onesf=onesm, cvec=cv, ksraw=ksr)


def _prep_core(x2d):
    x_tok = np.ascontiguousarray(x2d).reshape(NT, 128, D)
    x_feat = np.ascontiguousarray(x2d.T.reshape(ND, 128, S).transpose(1, 0, 2))
    return dict(x_tok=x_tok, x_feat=x_feat)


def kernel(L0_attn_hidden_states, L0_attn_wq, L0_attn_wk, L0_attn_wv, L0_attn_wo,
           L0_ffn_router_w, L0_ffn_A, L0_ffn_B):
    x = np.asarray(L0_attn_hidden_states, dtype=np.float32)
    shared = _prep_shared(
        np.asarray(L0_attn_wq, dtype=np.float32),
        np.asarray(L0_attn_wk, dtype=np.float32),
        np.asarray(L0_attn_wv, dtype=np.float32),
        np.asarray(L0_attn_wo, dtype=np.float32),
        np.asarray(L0_ffn_router_w, dtype=np.float32),
        np.asarray(L0_ffn_A, dtype=np.float32),
        np.asarray(L0_ffn_B, dtype=np.float32))

    nc = build()
    in_maps = []
    for b in range(8):
        m = dict(shared)
        m.update(_prep_core(x[b]))
        in_maps.append(m)
    res = run_bass_kernel_spmd(nc, in_maps, core_ids=list(range(8)))
    outs = [np.asarray(r["out"]).reshape(S, D) for r in res.results]
    return np.stack(outs, axis=0)
